# version 02 (final)
# baseline (speedup 1.0000x reference)
import numpy as np
import ml_dtypes
from contextlib import ExitStack

import concourse.bass as bass
import concourse.mybir as mybir
from concourse.bass_utils import run_bass_kernel_spmd

F32 = mybir.dt.float32
BF16 = mybir.dt.bfloat16
U8 = mybir.dt.uint8
AF = mybir.ActivationFunctionType
OP = mybir.AluOpType
AX = mybir.AxisListType
bf = ml_dtypes.bfloat16

NC = 8
T = 16384
TC = 2048
HALO = 128
TEXT = TC + HALO
NTILE = TEXT // 128
NTOWN = TC // 128
D = 1024
H = 4
DK = 256
DV = 256
CS = 32
SCK = 256
NSC = TC // SCK
NCH = SCK // CS
E = 8
INTER = 1024
SH_INTER = 2048
KALL = E * INTER + SH_INTER
GIN = D + 3 * DV
GH = 1024
MIN_LOCAL = 0.05

ENGINES = ("tensor", "vector", "scalar", "gpsimd", "sync")

NT_EXT = [(0, 512), (512, 512), (1024, 512), (1536, 512), (2048, 128)]
NT_OWN = [(0, 512), (512, 512), (1024, 512), (1536, 512)]

PHASES = ("P0", "B", "A", "DELTA", "C", "D")
SKIP_CC = False
DELTA_LIM = 3
DNSC = 8
P1X = set()


def ts(i, n=128):
    return slice(i * n, (i + 1) * n)


class Sched:
    def __init__(self, nc, st):
        self.nc = nc
        self.st = st
        self.block = st.enter_context(nc.Block())
        self.bar = st.enter_context(nc.semaphore("bar"))
        self.bar_count = 0
        self.sems = {}
        self.ctr = {}
        self.last = {e: None for e in ENGINES}

    def sem(self, name):
        if name not in self.sems:
            self.sems[name] = self.st.enter_context(self.nc.semaphore(name))
            self.ctr[name] = 0
        return self.sems[name]

    def section(self, engine):
        sched = self

        def deco(fn):
            def wrapper(eng):
                res = fn(eng)
                if res is not None:
                    sched.last[engine] = res
            getattr(self.block, engine)(wrapper)
        return deco

    def barrier(self):
        bar = self.bar
        fins = {e: self.last[e] for e in ENGINES}
        for e in ENGINES:
            self.last[e] = None
        self.bar_count += len(ENGINES)
        target = self.bar_count

        for e in ENGINES:
            fin = fins[e]

            def mk(eng_name, fin, tgt):
                def fn(eng):
                    if fin is not None:
                        sem_name, count = fin
                        eng.wait_ge(self.sems[sem_name], count)
                    eng.sem_inc(bar, 1)
                    eng.wait_ge(bar, tgt)
                getattr(self.block, eng_name)(fn)
            mk(e, fin, target)


def build_nc(debug=(), phases=PHASES):
    dbg = set(debug)
    phases = set(phases)
    nc = bass.Bass("TRN2", target_bir_lowering=False, debug=False)
    dram = {}

    def din(name, shape, dt=F32):
        dram[name] = nc.dram_tensor(name, shape, dt, kind="ExternalInput").ap()
        return dram[name]

    def dout(name, shape, dt=F32):
        dram[name] = nc.dram_tensor(name, shape, dt, kind="ExternalOutput").ap()
        return dram[name]

    def dint(name, shape, dt):
        dram[name] = nc.dram_tensor(name, shape, dt, kind="Internal").ap()
        return dram[name]

    x_ext = din("x_ext", [TEXT, D])
    wqT = din("wqT", [D, D])
    wkT = din("wkT", [D, D])
    wvT = din("wvT", [D, D])
    wbT = din("wbT", [D, H])
    qcd = din("qcd", [128, 8, 4, 128])
    kcd = din("kcd", [128, 8, 4, 128])
    vcd = din("vcd", [128, 8, 4, 128])
    fsd = din("fsd", [128, 8, 3, 128])
    fld = din("fld", [128, 8, 31, 128])
    fgw1T = din("fgw1T", [GIN, GH])
    fgb1 = din("fgb1", [128, 8])
    fgw2T = din("fgw2T", [GH, 4])
    fgb2 = din("fgb2", [1, 4])
    ltemp = din("ltemp", [1, 1])
    onw = din("onw", [128, 2])
    opT = din("opT", [D, D])
    gwT = din("gwT", [D, E])
    gb = din("gb", [1, E])
    w1T = din("w1T", [E, D, INTER])
    w3T = din("w3T", [E, D, INTER])
    w2T = din("w2T", [E, INTER, D])
    b1r = din("b1r", [128, 8, 8])
    b3r = din("b3r", [128, 8, 8])
    ws1T = din("ws1T", [D, SH_INTER])
    ws3T = din("ws3T", [D, SH_INTER])
    ws2T = din("ws2T", [SH_INTER, D])
    bs1r = din("bs1r", [128, 16])
    bs3r = din("bs3r", [128, 16])
    b2aug = din("b2aug", [E + 1, D])
    ident = din("ident", [128, 128], BF16)
    identf = din("identf", [128, 128])
    qeye = din("qeye", [128, 128])
    hsel = din("hsel", [128, 16, 8])
    trilm = din("trilm", [32, 32])
    trius = din("trius", [128, 32])
    eye4 = din("eye4", [128, 32])
    triuqk = din("triuqk", [128, 32])
    trils = din("trils", [128, 32])
    onehot = din("onehot", [128, 8])
    cmask = din("cmask", [128, 2048])

    out = dout("out", [TC, D])

    qf_h = dint("qf_h", [D, TC], BF16)
    kf_h = dint("kf_h", [D, TC], BF16)
    vf_h = dint("vf_h", [D, TC + 32], BF16)
    sh_h = dint("sh_h", [D, TC], BF16)
    lg_h = dint("lg_h", [D, TC], BF16)
    dl_h = dint("dl_h", [D, TC], BF16)
    hp_h = dint("hp_h", [KALL, TC], BF16)
    w2c16 = dint("w2c16", [8, KALL, 128], BF16)
    g16h = dint("g16h", [9, TC], F32)
    sp_h = dint("sp_h", [8, TC], F32)
    dl_tm_h = dint("dl_tm_h", [TC, D], BF16)
    fgx_h = dint("fgx_h", [4 * D, GH], BF16)
    st_h = dint("st_h", [TC, 16], F32)
    CCW = DV + DK
    cc_in = dint("cc_in", [H * 2 * 128 * CCW], F32)
    cc_out = dint("cc_out", [NC * H * 2 * 128 * CCW], F32)

    with ExitStack() as est:
        def sb(name, shape, dt):
            return est.enter_context(nc.sbuf_tensor(name, shape, dt))

        def pt(name, shape, dt=F32):
            return est.enter_context(nc.psum_tensor(name, shape, dt))

        S = Sched(nc, est)
        LD = S.sem("LD")
        LG = S.sem("LG")
        LS = S.sem("LS")
        STO = S.sem("STO")
        MM = S.sem("MM")
        VV = S.sem("VV")
        AA = S.sem("AA")
        GG = S.sem("GG")
        CCs = S.sem("CCs")
        XS = [S.sem("XSa"), S.sem("XSb")]
        OS = [S.sem("OSa"), S.sem("OSb")]

        def cnt(sem_name, amt=0):
            if amt:
                S.ctr[sem_name] += amt
            return S.ctr[sem_name]

        arena = sb("arena", [128, 33792], F32)
        xn = arena[:, 0:8704].bitcast(BF16).rearrange("p (c t) -> p c t", c=8)
        smean = arena[:, 8704:9728].bitcast(BF16).rearrange("p (c t) -> p c t", c=2)
        lmean = arena[:, 9728:10752].bitcast(BF16).rearrange("p (c t) -> p c t", c=2)
        dmean = arena[:, 10752:11776].bitcast(BF16).rearrange("p (c t) -> p c t", c=2)
        POOL = 14848
        pool = arena[:, POOL:]

        acc = sb("acc", [128, 8, TC], F32)
        id16 = sb("id16", [128, 128], BF16)
        idf = sb("idf", [128, 128], F32)
        qeye_s = sb("qeye_s", [128, 128], BF16)
        hsel_s = sb("hsel_s", [128, 16, 8], BF16)
        trilm_s = sb("trilm_s", [32, 32], F32)

        fgb1_s = sb("fgb1_s", [128, 8], F32)
        fgw2_s = sb("fgw2_s", [128, 8, 4], BF16)
        fgb2_s = sb("fgb2_s", [1, 4], F32)
        gw_s = sb("gw_s", [128, 8, 8], BF16)
        gb_s = sb("gb_s", [1, 8], F32)
        b1_s = sb("b1_s", [128, 8, 8], F32)
        b3_s = sb("b3_s", [128, 8, 8], F32)
        bs1_s = sb("bs1_s", [128, 16], F32)
        bs3_s = sb("bs3_s", [128, 16], F32)
        b2a_s = sb("b2a_s", [E + 1, D], F32)
        onw_s = sb("onw_s", [128, 2], F32)
        lt_s = sb("lt_s", [1, 4], F32)
        ones_s = sb("ones_s", [1, 128], BF16)
        wb_s = sb("wb_s", [128, 8, H], BF16)
        beta_tm = sb("beta_tm", [128, 16, H], F32)
        spk = sb("spk", [128, 16, 4], F32)
        ppk = sb("ppk", [128, 16, 4], F32)
        bspk = sb("bspk", [128, 16, 4], F32)
        nspk = sb("nspk", [128, 16, 4], F32)


        ps = [pt(f"ps{i}", [128, 512], F32) for i in range(7)]
        pt16 = [pt("pt16a", [128, 1024], BF16)]

        dbg_aps = {}
        if "xn" in dbg:
            dbg_aps["xn"] = dout("dbg_xn", [8, 128, TEXT], BF16)
        if "gate" in dbg:
            dbg_aps["gate"] = dout("dbg_gate", [128, 16, 8])
        if "ffn" in dbg:
            dbg_aps["ffn"] = dout("dbg_ffn", [8, 128, TC])
        if "cgate" in dbg:
            dbg_aps["cgate"] = dout("dbg_cgate", [128, 16, 4])
        if "delta" in dbg:
            dbg_aps["delta"] = dout("dbg_delta", [TC, D], BF16)
            dbg_aps["tbt"] = dout("dbg_tbt", [128, 2048], BF16)
            dbg_aps["xs"] = dout("dbg_xs", [128, 4096])
            dbg_aps["spk"] = dout("dbg_spk", [128, 16, 4])
            dbg_aps["ppk2"] = dout("dbg_ppk2", [128, 16, 4])
            dbg_aps["bspk"] = dout("dbg_bspk", [128, 16, 4])
            dbg_aps["beta2"] = dout("dbg_beta2", [128, 16, 4])
        if "qkv" in dbg:
            dbg_aps["q"] = dout("dbg_q", [D, TC], BF16)
            dbg_aps["k"] = dout("dbg_k", [D, TC], BF16)
            dbg_aps["v"] = dout("dbg_v", [D, TC + 32], BF16)
            dbg_aps["beta"] = dout("dbg_beta", [128, 16, H])
            dbg_aps["stats"] = dout("dbg_stats", [8, TC])
            dbg_aps["short"] = dout("dbg_short", [D, TC], BF16)
            dbg_aps["long"] = dout("dbg_long", [D, TC], BF16)
            dbg_aps["packs"] = dout("dbg_packs", [128, 16, H])
            dbg_aps["packp"] = dout("dbg_packp", [128, 16, H])

        @S.section("sync")
        def _(sy):
            last = None
            for dst, src in [(id16, ident), (idf, identf), (trilm_s, trilm),
                             (b2a_s, b2aug),
                             (fgb1_s, fgb1), (fgb2_s, fgb2),
                             (gb_s, gb), (b1_s, b1r), (b3_s, b3r),
                             (bs1_s, bs1r), (bs3_s, bs3r), (onw_s, onw)]:
                last = sy.dma_start(out=dst[:], in_=src[:])
                last.then_inc(LD, 16)
                cnt("LD", 16)
            last = sy.dma_start(out=lt_s[:, 0:1], in_=ltemp[:])
            last.then_inc(LD, 16)
            cnt("LD", 16)
            return ("LD", cnt("LD"))

        @S.section("gpsimd")
        def _(g):
            last = None
            for dst, src in [
                (qeye_s[:], qeye[:]),
                (hsel_s[:], hsel[:]),
                (fgw2_s[:], fgw2T[:].rearrange("(c p) f -> p c f", p=128)),
                (gw_s[:], gwT[:].rearrange("(c p) f -> p c f", p=128)),
                (wb_s[:], wbT[:].rearrange("(c p) f -> p c f", p=128)),
            ]:
                last = g.dma_start(out=dst, in_=src)
                last.then_inc(LG, 16)
                cnt("LG", 16)
            g.memset(ones_s[:], 1.0)
            last = g.memset(acc[:], 0.0)
            last.then_inc(GG, 1)
            cnt("GG", 1)
            return ("GG", cnt("GG"))

        S.ld_const = cnt("LD")
        S.lg_const = cnt("LG")

        xbuf = pool[:, 0:2048].rearrange("p (b t) -> p b t", b=2)
        xnt = pool[:, 2048:3072].bitcast(BF16).rearrange("p (b t) -> p b t", b=2)
        sstat = pool[:, 3072:3104].rearrange("p (b c) -> p b c", b=2)

        vv0 = cnt("VV")
        aa0 = cnt("AA")
        mm0 = cnt("MM")
        xs0 = [cnt("XSa"), cnt("XSb")]

        @S.section("sync")
        def _(sy):
            last = None
            for i in range(NTILE):
                b = i % 2
                if i >= 2:
                    sy.wait_ge(VV, vv0 + 10 * (i - 2) + 2)
                last = sy.dma_start(out=xbuf[:, b, :], in_=x_ext[ts(i), :])
                last.then_inc(XS[b], 16)
                cnt("XS" + "ab"[b], 16)
            return ("XS" + "ab"[(NTILE - 1) % 2], cnt("XS" + "ab"[(NTILE - 1) % 2]))

        @S.section("vector")
        def _(v):
            last = None
            for i in range(NTILE):
                b = i % 2
                v.wait_ge(XS[b], xs0[b] + 16 * (i // 2 + 1))
                xb = xbuf[:, b, :]
                mv = sstat[:, b, 12:14]
                m = sstat[:, b, 12:13]
                e2 = sstat[:, b, 8:9]
                v.bn_stats(sstat[:, b, 0:6], xb[:, 0:512])
                v.bn_stats(sstat[:, b, 6:12], xb[:, 512:1024])
                v.drain()
                v.bn_aggr(mv, sstat[:, b, 0:12])
                v.drain()
                last = v.tensor_scalar(e2, sstat[:, b, 13:14], 1e-5, None, OP.add)
                last.then_inc(VV, 1)
                v.wait_ge(AA, aa0 + i + 1)
                if i >= 2:
                    v.wait_ge(MM, mm0 + 8 * (i - 1))
                rs = sstat[:, b, 9:10]
                last = v.tensor_scalar(xnt[:, b, :], xb, m, rs,
                                       OP.subtract, op1=OP.mult)
                last.then_inc(VV, 1)
                for c in range(8):
                    v.wait_ge(MM, mm0 + 8 * i + c + 1)
                    last = v.tensor_copy(xn[:, c, ts(i)], pt16[0][:, 0:128])
                    last.then_inc(VV, 1)
            cnt("VV", 10 * NTILE)
            return ("VV", cnt("VV"))

        @S.section("scalar")
        def _(s):
            last = None
            for i in range(NTILE):
                b = i % 2
                s.wait_ge(VV, vv0 + 10 * i + 1)
                e2 = sstat[:, b, 8:9]
                rs = sstat[:, b, 9:10]
                s.activation(rs, e2, AF.Ln)
                s.drain()
                last = s.activation(rs, rs, AF.Exp, scale=-0.5)
                last.then_inc(AA, 1)
            cnt("AA", NTILE)
            return ("AA", cnt("AA"))

        @S.section("tensor")
        def _(t):
            last = None
            t.wait_ge(LD, S.ld_const)
            t.wait_ge(LG, S.lg_const)
            for i in range(NTILE):
                b = i % 2
                t.wait_ge(VV, vv0 + 10 * i + 2)
                for c in range(8):
                    if 8 * i + c >= 1:
                        j = 8 * i + c - 1
                        t.wait_ge(VV, vv0 + 10 * (j // 8) + 3 + (j % 8))
                    last = t.transpose(pt16[0][:, 0:128], xnt[:, b, ts(c)], id16[:])
                    last.then_inc(MM, 1)
            cnt("MM", 8 * NTILE)
            return ("MM", cnt("MM"))

        S.barrier()

        if "xn" in dbg:
            @S.section("sync")
            def _(sy):
                last = None
                for c in range(8):
                    last = sy.dma_start(out=dbg_aps["xn"][c], in_=xn[:, c, :])
                    last.then_inc(STO, 16)
                    cnt("STO", 16)
                return ("STO", cnt("STO"))
            S.barrier()

        if "B" in phases:
            glog = pool[:, 0:128].rearrange("p (a b) -> p a b", a=16)
            gsc = pool[:, 128:256].rearrange("p (a b) -> p a b", a=16)
            gneg = pool[:, 256:384].rearrange("p (a b) -> p a b", a=16)
            gm = pool[:, 384:512].rearrange("p (a b) -> p a b", a=16)
            gmsk = pool[:, 512:544].bitcast(U8).rearrange(
                "p (a b) -> p a b", a=16)
            gtb = sb("gtb", [8, 2, 128], F32)
            g16h = dram["g16h"]
            ones_f = sb("ones_f", [1, 128], F32)

            mmb0 = cnt("MM")
            vvb0 = cnt("VV")

            @S.section("gpsimd")
            def _(g):
                g.memset(ones_f[:], 1.0)
                last = g.memset(gneg[:, :, :], -1.0)
                last.then_inc(GG, 1)
                cnt("GG", 1)
                return ("GG", cnt("GG"))

            @S.section("tensor")
            def _(t):
                last = None
                t.wait_ge(GG, cnt("GG"))
                for i in range(NTOWN):
                    pp = ps[2 + i % 2]
                    if i >= 2:
                        t.wait_ge(VV, vvb0 + (i - 2) + 1)
                    for kc in range(8):
                        t.matmul(pp[:, 0:8], xn[:, kc, ts(i + 1)],
                                 gw_s[:, kc, :], start=(kc == 0), stop=False)
                    last = t.matmul(pp[:, 0:8], ones_f[:, :], gb_s[:, :],
                                    start=False, stop=True)
                    last.then_inc(MM, 1)
                cnt("MM", NTOWN)
                return ("MM", cnt("MM"))

            @S.section("vector")
            def _(v):
                last = None
                for i in range(NTOWN):
                    v.wait_ge(MM, mmb0 + i + 1)
                    last = v.tensor_copy(glog[:, i, :], ps[2 + i % 2][:, 0:8])
                    last.then_inc(VV, 1)
                cnt("VV", NTOWN)
                v.drain()
                v.tensor_reduce(gm[:, :, 0], glog[:, :, :], AX.X, OP.max)
                v.drain()
                a0, b0 = bass.broadcast_tensor_aps(glog[:, :, :], gm[:, :, 0:1])
                last = v.tensor_tensor(gsc[:, :, :], a0, b0, OP.subtract)
                last.then_inc(VV, 1)
                cnt("VV", 1)
                return ("VV", cnt("VV"))

            @S.section("scalar")
            def _(s):
                s.wait_ge(VV, cnt("VV"))
                last = s.activation(gsc[:, :, :], gsc[:, :, :], AF.Exp)
                last.then_inc(AA, 1)
                cnt("AA", 1)
                return ("AA", cnt("AA"))

            mmb1 = cnt("MM")
            vvb1pre = cnt("VV")
            GT = [S.sem("GTa"), S.sem("GTb")]
            gt_b0 = [cnt("GTa"), cnt("GTb")]

            @S.section("vector")
            def _(v):
                v.wait_ge(AA, cnt("AA"))
                v.tensor_reduce(gm[:, :, 1], gsc[:, :, :], AX.X, OP.add)
                v.drain()
                v.reciprocal(gm[:, :, 2], gm[:, :, 1:2])
                v.drain()
                a0, b0 = bass.broadcast_tensor_aps(gsc[:, :, :], gm[:, :, 2:3])
                v.tensor_tensor(glog[:, :, :], a0, b0, OP.mult)
                v.drain()
                v.tensor_reduce(gm[:, :, 3], glog[:, :, :], AX.X, OP.max)
                v.drain()
                a0, b0 = bass.broadcast_tensor_aps(glog[:, :, :], gm[:, :, 3:4])
                v.tensor_tensor(gmsk[:, :, :], a0, b0, OP.is_ge)
                v.drain()
                v.select(gsc[:, :, :], gmsk[:, :, :], gneg[:, :, :],
                         glog[:, :, :], add_drain=True)
                v.drain()
                v.tensor_reduce(gm[:, :, 4], gsc[:, :, :], AX.X, OP.max)
                v.drain()
                a0, b0 = bass.broadcast_tensor_aps(glog[:, :, :], gm[:, :, 4:5])
                v.tensor_tensor(gmsk[:, :, :], a0, b0, OP.is_ge)
                v.drain()
                last = v.tensor_tensor(glog[:, :, :], glog[:, :, :],
                                       gmsk[:, :, :], OP.mult)
                last.then_inc(VV, 1)
                cnt("VV", 1)
                for i in range(NTOWN):
                    v.wait_ge(MM, mmb1 + i + 1)
                    if i >= 2:
                        v.wait_ge(GT[i % 2], gt_b0[i % 2] +
                                  32 * ((i - 2) // 2 + 1))
                    last = v.tensor_copy(gtb[0:8, i % 2, :],
                                         ps[4 + i % 2][0:8, 0:128])
                    last.then_inc(VV, 1)
                cnt("VV", NTOWN)
                return ("VV", cnt("VV"))

            @S.section("tensor")
            def _(t):
                last = None
                t.wait_ge(VV, vvb1pre + 1)
                for i in range(NTOWN):
                    if i >= 2:
                        t.wait_ge(VV, vvb1pre + 1 + (i - 2) + 1)
                    last = t.transpose(ps[4 + i % 2][0:8, 0:128],
                                       glog[:, i, :], idf[:])
                    last.then_inc(MM, 1)
                cnt("MM", NTOWN)
                return ("MM", cnt("MM"))

            @S.section("sync")
            def _(sy):
                last = None
                sy.wait_ge(GG, cnt("GG"))
                for i in range(NTOWN):
                    sl = i % 2
                    sy.wait_ge(VV, vvb1pre + 1 + i + 1)
                    last = sy.dma_start(out=g16h[0:8, ts(i)],
                                        in_=gtb[:, sl, :])
                    last.then_inc(GT[sl], 16)
                    cnt("GT" + "ab"[sl], 16)
                    last = sy.dma_start(out=g16h[8:9, ts(i)],
                                        in_=ones_f[:, :])
                    last.then_inc(GT[sl], 16)
                    cnt("GT" + "ab"[sl], 16)
                sl = (NTOWN - 1) % 2
                sy.wait_ge(GT[1 - sl], cnt("GT" + "ab"[1 - sl]))
                return ("GT" + "ab"[sl], cnt("GT" + "ab"[sl]))

            S.barrier()
            if "gate" in dbg:
                @S.section("sync")
                def _(sy):
                    last = sy.dma_start(out=dbg_aps["gate"], in_=glog[:, :, :])
                    last.then_inc(STO, 16)
                    cnt("STO", 16)
                    return ("STO", cnt("STO"))
                S.barrier()

            wsl = pool[:, 1024:5120].bitcast(BF16).rearrange(
                "p (s w kc f) -> p s w kc f", s=2, w=2, kc=8)
            s1b = pool[:, 9216:9728].bitcast(BF16).rearrange(
                "p (s f) -> p s f", s=2)
            s3b = pool[:, 9728:10240].bitcast(BF16).rearrange(
                "p (s f) -> p s f", s=2)
            hpb = pool[:, 10240:10752].bitcast(BF16).rearrange(
                "p (s f) -> p s f", s=2)
            gbb = pool[:, 10752:11776].rearrange(
                "p (s f) -> p s f", s=2)
            WS = [S.sem("WSa"), S.sem("WSb")]
            HS = [S.sem("HSa"), S.sem("HSb")]
            H2 = [S.sem("H2a"), S.sem("H2b")]
            GB = [S.sem("GBa"), S.sem("GBb")]

            subs = []
            for e in range(E):
                for q in range(4):
                    subs.append((w1T[e][:, ts(q, 256)], w3T[e][:, ts(q, 256)],
                                 ("b", e, q * 2), e, e * INTER + q * 256))
            for u in range(2):
                for q in range(4):
                    off = u * 1024 + q * 256
                    subs.append((ws1T[:, off:off + 256], ws3T[:, off:off + 256],
                                 ("bs", None, u * 8 + q * 2), None,
                                 E * INTER + off))
            NSU = len(subs)
            mmb2 = cnt("MM")
            aab2 = cnt("AA")
            ws_b0 = [cnt("WSa"), cnt("WSb")]
            hs_b0 = [cnt("HSa"), cnt("HSb")]
            gb_b0 = [cnt("GBa"), cnt("GBb")]

            def mm_h(su, nt, m, which):
                return mmb2 + 16 * su + 4 * nt + 2 * m + which

            def mu_of(su, nt, m):
                return su * 8 + nt * 2 + m

            vv_hp = {}

            vvb3 = cnt("VV")
            S.gb_consumed = {}
            gidx = 0
            for su, sd in enumerate(subs):
                if sd[3] is None:
                    continue
                for nt in range(4):
                    S.gb_consumed[gidx] = vvb3 + mu_of(su, nt, 1) + 1
                    gidx += 1

            @S.section("vector")
            def _(v):
                last = None
                gidx = 0
                for su, (w1s, w3s, bcol, grow, r0) in enumerate(subs):
                    for nt in range(4):
                        if grow is not None:
                            sl = gidx % 2
                            v.wait_ge(GB[sl], gb_b0[sl] +
                                      16 * (gidx // 2 + 1))
                            gcur = sl
                            gidx += 1
                        for m in range(2):
                            mu = mu_of(su, nt, m)
                            hsl_ = mu % 2
                            v.wait_ge(AA, aab2 + 2 * mu + 2)
                            if mu >= 2:
                                v.wait_ge(HS[hsl_], hs_b0[hsl_] +
                                          16 * ((mu - 2) // 2 + 1))
                            if grow is not None:
                                v.tensor_tensor(hpb[:, hsl_, :],
                                                s1b[:, hsl_, :],
                                                s3b[:, hsl_, :], OP.mult)
                                v.drain()
                                last = v.tensor_tensor(hpb[:, hsl_, :],
                                                       hpb[:, hsl_, :],
                                                       gbb[:, gcur, :],
                                                       OP.mult)
                            else:
                                last = v.tensor_tensor(hpb[:, hsl_, :],
                                                       s1b[:, hsl_, :],
                                                       s3b[:, hsl_, :],
                                                       OP.mult)
                            last.then_inc(VV, 1)
                            vv_hp[mu] = cnt("VV", 1)
                return ("VV", cnt("VV"))

            @S.section("tensor")
            def _(t):
                last = None
                for su, (w1s, w3s, bcol, grow, r0) in enumerate(subs):
                    sl = su % 2
                    t.wait_ge(WS[sl], ws_b0[sl] + 32 * (su // 2 + 1))
                    for nt in range(4):
                        for m in range(2):
                            mu = mu_of(su, nt, m)
                            p1 = ps[2 + mu % 2]
                            p3 = ps[4 + mu % 2]
                            if mu >= 2:
                                t.wait_ge(AA, aab2 + 2 * (mu - 2) + 2)
                            for kc in range(8):
                                last = t.matmul(
                                    p1[:, 0:512],
                                    wsl[:, sl, 0, kc, ts(m, 128)],
                                    xn[:, kc, HALO + nt * 512:
                                       HALO + (nt + 1) * 512],
                                    start=(kc == 0), stop=(kc == 7))
                            last.then_inc(MM, 1)
                            cnt("MM", 1)
                            for kc in range(8):
                                last = t.matmul(
                                    p3[:, 0:512],
                                    wsl[:, sl, 1, kc, ts(m, 128)],
                                    xn[:, kc, HALO + nt * 512:
                                       HALO + (nt + 1) * 512],
                                    start=(kc == 0), stop=(kc == 7))
                            last.then_inc(MM, 1)
                            cnt("MM", 1)
                return ("MM", cnt("MM"))

            @S.section("scalar")
            def _(s):
                last = None
                for su, (w1s, w3s, bcol, grow, r0) in enumerate(subs):
                    for nt in range(4):
                        for m in range(2):
                            mu = mu_of(su, nt, m)
                            hsl_ = mu % 2
                            p1 = ps[2 + mu % 2]
                            p3 = ps[4 + mu % 2]
                            kind, e, c0 = bcol
                            if kind == "b":
                                bb1 = b1_s[:, e, c0 + m:c0 + m + 1]
                                bb3 = b3_s[:, e, c0 + m:c0 + m + 1]
                            else:
                                bb1 = bs1_s[:, c0 + m:c0 + m + 1]
                                bb3 = bs3_s[:, c0 + m:c0 + m + 1]
                            s.wait_ge(MM, mm_h(su, nt, m, 1))
                            if mu >= 2:
                                s.wait_ge(VV, vv_hp[mu - 2])
                            last = s.activation(s1b[:, hsl_, :], p1[:, 0:512],
                                                AF.Silu, bias=bb1)
                            last.then_inc(AA, 1)
                            cnt("AA", 1)
                            s.wait_ge(MM, mm_h(su, nt, m, 2))
                            last = s.activation(s3b[:, hsl_, :], p3[:, 0:512],
                                                AF.Identity, bias=bb3)
                            last.then_inc(AA, 1)
                            cnt("AA", 1)
                return ("AA", cnt("AA"))

            @S.section("sync")
            def _(sy):
                last = None
                gidx = 0
                for su, (w1s, w3s, bcol, grow, r0) in enumerate(subs):
                    for nt in range(4):
                        if grow is not None:
                            sl = gidx % 2
                            if gidx >= 2:
                                sy.wait_ge(VV, S.gb_consumed[gidx - 2])
                            base = g16h[grow:grow + 1,
                                        nt * 512:(nt + 1) * 512]
                            bc = bass.AP(base.tensor, base.offset,
                                         [[0, 128], [1, 512]])
                            last = sy.dma_start(out=gbb[:, sl, :], in_=bc)
                            last.then_inc(GB[sl], 16)
                            cnt("GB" + "ab"[sl], 16)
                            gidx += 1
                        for m in range(2):
                            mu = mu_of(su, nt, m)
                            hsl_ = mu % 2
                            sy.wait_ge(VV, vv_hp[mu])
                            last = sy.dma_start(
                                out=hp_h[r0 + m * 128: r0 + (m + 1) * 128,
                                         ts(nt, 512)],
                                in_=hpb[:, hsl_, :])
                            last.then_inc(HS[hsl_], 16)
                            cnt("HS" + "ab"[hsl_], 16)
                sl = (NSU * 8 - 1) % 2
                sy.wait_ge(HS[1 - sl], cnt("HS" + "ab"[1 - sl]))
                return ("HS" + "ab"[sl], cnt("HS" + "ab"[sl]))

            @S.section("gpsimd")
            def _(g):
                last = None
                for su, (w1s, w3s, bcol, grow, r0) in enumerate(subs):
                    sl = su % 2
                    if su >= 2:
                        g.wait_ge(MM, mmb2 + 16 * (su - 1))
                    last = g.dma_start(
                        out=wsl[:, sl, 0, :, :],
                        in_=w1s.rearrange("(kc p) f -> p kc f", p=128))
                    last.then_inc(WS[sl], 16)
                    cnt("WS" + "ab"[sl], 16)
                    last = g.dma_start(
                        out=wsl[:, sl, 1, :, :],
                        in_=w3s.rearrange("(kc p) f -> p kc f", p=128))
                    last.then_inc(WS[sl], 16)
                    cnt("WS" + "ab"[sl], 16)
                sl = (NSU - 1) % 2
                g.wait_ge(WS[1 - sl], cnt("WS" + "ab"[1 - sl]))
                return ("WS" + "ab"[sl], cnt("WS" + "ab"[sl]))

            S.barrier()

        def emit_B2():
            w2c16 = dram["w2c16"]
            g16h = dram["g16h"]

            @S.section("gpsimd")
            def _(g):
                last = None
                for m2 in range(8):
                    last = g.dma_start(
                        out=w2c16[m2][0:E * INTER, :],
                        in_=w2T.rearrange("e i d -> (e i) d")[:, ts(m2)])
                    last.then_inc(LG, 16)
                    cnt("LG", 16)
                    last = g.dma_start(
                        out=w2c16[m2][E * INTER:KALL, :],
                        in_=ws2T[:, ts(m2)])
                    last.then_inc(LG, 16)
                    cnt("LG", 16)
                return ("LG", cnt("LG"))

            S.barrier()
            hsl = arena[:, 0:20480].bitcast(BF16).rearrange(
                "p (s kc f) -> p s kc f", s=2, kc=80)
            w2sl = arena[:, 20480:30720].bitcast(BF16).rearrange(
                "p (s kc f) -> p s kc f", s=2, kc=80)
            g2b = sb("g2b", [9, 2, 256], F32)
            mmB = cnt("MM")
            vvB = cnt("VV")
            ws_c0 = [cnt("WSa"), cnt("WSb")]
            h2_c0 = [cnt("H2a"), cnt("H2b")]
            gb_c0 = [cnt("GBa"), cnt("GBb")]

            def uidx(m2, nt):
                return m2 * 8 + nt

            @S.section("gpsimd")
            def _(g):
                last = None
                for m2 in range(8):
                    sl = m2 % 2
                    if m2 >= 2:
                        g.wait_ge(MM, mmB + 8 * (m2 - 1))
                    last = g.dma_start(
                        out=w2sl[:, sl, :, :],
                        in_=w2c16[m2].rearrange("(kc p) f -> p kc f", p=128))
                    last.then_inc(WS[sl], 16)
                    cnt("WS" + "ab"[sl], 16)
                sl = 7 % 2
                g.wait_ge(WS[1 - sl], cnt("WS" + "ab"[1 - sl]))
                return ("WS" + "ab"[sl], cnt("WS" + "ab"[sl]))

            @S.section("sync")
            def _(sy):
                last = None
                for m2 in range(8):
                    for nt in range(8):
                        u = uidx(m2, nt)
                        sl = u % 2
                        if u >= 2:
                            sy.wait_ge(MM, mmB + (u - 2) + 1)
                        last = sy.dma_start(
                            out=hsl[:, sl, :, :],
                            in_=hp_h.rearrange("(kc p) t -> p kc t", p=128)
                            [:, :, ts(nt, 256)])
                        last.then_inc(H2[sl], 16)
                        cnt("H2" + "ab"[sl], 16)
                sl = 63 % 2
                sy.wait_ge(H2[1 - sl], cnt("H2" + "ab"[1 - sl]))
                return ("H2" + "ab"[sl], cnt("H2" + "ab"[sl]))

            @S.section("scalar")
            def _(s):
                last = None
                for m2 in range(8):
                    for nt in range(8):
                        u = uidx(m2, nt)
                        sl = u % 2
                        if u >= 2:
                            s.wait_ge(MM, mmB + (u - 2) + 1)
                        last = s.dma_start(
                            out=g2b[:, sl, :],
                            in_=g16h[:, ts(nt, 256)])
                        last.then_inc(GB[sl], 16)
                        cnt("GB" + "ab"[sl], 16)
                sl = 63 % 2
                s.wait_ge(GB[1 - sl], cnt("GB" + "ab"[1 - sl]))
                return ("GB" + "ab"[sl], cnt("GB" + "ab"[sl]))

            @S.section("tensor")
            def _(t):
                last = None
                for m2 in range(8):
                    wl = m2 % 2
                    t.wait_ge(WS[wl], ws_c0[wl] + 16 * (m2 // 2 + 1))
                    for nt in range(8):
                        u = uidx(m2, nt)
                        sl = u % 2
                        pp = ps[2 + u % 2]
                        t.wait_ge(H2[sl], h2_c0[sl] + 16 * (u // 2 + 1))
                        t.wait_ge(GB[sl], gb_c0[sl] + 16 * (u // 2 + 1))
                        if u >= 2:
                            t.wait_ge(VV, vvB + (u - 2) + 1)
                        for kc in range(80):
                            t.matmul(pp[:, 0:256], w2sl[:, wl, kc, :],
                                     hsl[:, sl, kc, :],
                                     start=(kc == 0), stop=False)
                        last = t.matmul(pp[:, 0:256], b2a_s[:, ts(m2)],
                                        g2b[:, sl, :],
                                        start=False, stop=True)
                        last.then_inc(MM, 1)
                        cnt("MM", 1)
                return ("MM", cnt("MM"))

            @S.section("vector")
            def _(v):
                last = None
                for m2 in range(8):
                    for nt in range(8):
                        u = uidx(m2, nt)
                        pp = ps[2 + u % 2]
                        v.wait_ge(MM, mmB + u + 1)
                        last = v.tensor_tensor(
                            acc[:, m2, ts(nt, 256)], pp[:, 0:256],
                            acc[:, m2, ts(nt, 256)], OP.add)
                        last.then_inc(VV, 1)
                        cnt("VV", 1)
                return ("VV", cnt("VV"))

            S.barrier()

        if "B" in phases and "A" not in phases:
            emit_B2()
            if "ffn" in dbg:
                @S.section("sync")
                def _(sy):
                    last = None
                    for c in range(8):
                        last = sy.dma_start(out=dbg_aps["ffn"][c],
                                            in_=acc[:, c, :])
                        last.then_inc(STO, 16)
                        cnt("STO", 16)
                    return ("STO", cnt("STO"))
                S.barrier()

        if "A" in phases:
            prebuf = pool[:, 0:2184].bitcast(BF16).rearrange(
                "p (s f) -> p s f", s=2)
            convd = pool[:, 2184:4232].bitcast(BF16).rearrange(
                "p (c t f) -> p c t f", c=8, t=4)
            firdb = pool[:, 4232:6216].bitcast(BF16).rearrange(
                "p (s t f) -> p s t f", s=1, t=31)
            outb = pool[:, 6216:8400].bitcast(BF16).rearrange(
                "p (s f) -> p s f", s=2)
            sqb = pool[:, 8400:9424].bitcast(BF16)
            lgb = pool[:, 9424:10448].bitcast(BF16)
            stats = pool[:, 10448:12496]
            wproj = pool[:, 12496:16592].bitcast(BF16).rearrange(
                "p (c f) -> p c f", c=8)
            fsd_s = pool[:, 16592:18128].bitcast(BF16).rearrange(
                "p (c t f) -> p c t f", c=8, t=3)

            if "WSa" not in S.sems:
                WS = [S.sem("WSa"), S.sem("WSb")]
            else:
                WS = [S.sems["WSa"], S.sems["WSb"]]
            FS = [S.sem("FSa"), S.sem("FSb")]
            OBS = [S.sem("OBa"), S.sem("OBb")]
            KPS = [S.sem("KPa"), S.sem("KPb")]

            WSRC = {"q": wqT, "k": wkT, "v": wvT}
            CSRC = {"q": qcd, "k": kcd, "v": vcd}

            @S.section("gpsimd")
            def _(g):
                g.memset(prebuf[:, 0, 0:4], 0.0)
                last = g.memset(prebuf[:, 1, 0:4], 0.0)
                last.then_inc(GG, 1)
                cnt("GG", 1)
                last = g.dma_start(out=fsd_s[:], in_=fsd[:])
                last.then_inc(LG, 16)
                cnt("LG", 16)
                return ("LG", cnt("LG"))

            def subphase_proj(y):
                mm_marks = {}
                vv_marks = {}
                aa_marks = {}
                mm0 = cnt("MM")
                lg0 = cnt("LG")
                ob0 = [cnt("OBa"), cnt("OBb")]
                fs0 = [cnt("FSa"), cnt("FSb")]

                aa0 = cnt("AA")
                vv0 = cnt("VV")
                for c in range(8):
                    for ni in range(5):
                        aa_marks[("pre", c, ni)] = aa0 + 10 * c + ni + 1
                    for ni in range(5):
                        aa_marks[("out", c, ni)] = aa0 + 10 * c + 5 + ni + 1
                    if y == "v":
                        for f in range(8):
                            vv_marks[("fir", c, f)] = vv0 + 8 * c + f + 1
                    else:
                        vv_marks[("sq", c)] = vv0 + c + 1
                def mm_proj(c, ni):
                    per = 10 + (8 if y == "v" else (4 if c == 7 else 0))
                    return None
                mm_cum = mm0
                for c in range(8):
                    for ni in range(5):
                        mm_cum += 1
                        mm_marks[("proj", c, ni)] = mm_cum
                    for ni in range(5):
                        mm_cum += 1
                        mm_marks[("conv", c, ni)] = mm_cum
                    if y == "v":
                        for f in range(8):
                            mm_cum += 1
                            mm_marks[("fir", c, f)] = mm_cum
                    elif c == 7:
                        for ni in range(4):
                            mm_cum += 1
                            mm_marks[("stat", ni)] = mm_cum

                @S.section("gpsimd")
                def _(g):
                    last = g.dma_start(out=convd[:, :, :, :], in_=CSRC[y][:])
                    last.then_inc(LG, 16)
                    cnt("LG", 16)
                    last = g.dma_start(
                        out=wproj[:, :, :],
                        in_=WSRC[y].rearrange("(kc p) f -> p kc f", p=128))
                    last.then_inc(LG, 16)
                    cnt("LG", 16)
                    if y == "v":
                        for c in range(8):
                            if c >= 1:
                                g.wait_ge(MM, mm_marks[("fir", c - 1, 7)])
                            last = g.dma_start(
                                out=firdb[:, 0, :, :],
                                in_=fld[:, c, :, :])
                            last.then_inc(FS[0], 16)
                            cnt("FSa", 16)
                    return ("LG", cnt("LG"))

                @S.section("tensor")
                def _(t):
                    last = None
                    t.wait_ge(LG, lg0 + 32)
                    for c in range(8):
                        slot = c % 2
                        if c >= 2:
                            t.wait_ge(AA, aa_marks[("out", c - 2, 4)])
                        for ni, (t0, tw) in enumerate(NT_EXT):
                            pp = ps[4]
                            if (c, ni) != (0, 0):
                                pni, pc = (ni - 1, c) if ni > 0 else (4, c - 1)
                                t.wait_ge(AA, aa_marks[("pre", pc, pni)])
                            for kc in range(8):
                                last = t.matmul(pp[:, 0:tw],
                                                wproj[:, kc, ts(c)],
                                                xn[:, kc, t0:t0 + tw],
                                                start=(kc == 0),
                                                stop=(kc == 7))
                            last.then_inc(MM, 1)
                            cnt("MM", 1)
                        for ni, (t0, tw) in enumerate(NT_EXT):
                            pp = ps[5]
                            t.wait_ge(AA, aa_marks[("pre", c, ni)])
                            if (c, ni) != (0, 0):
                                pni, pc = (ni - 1, c) if ni > 0 else (4, c - 1)
                                t.wait_ge(AA, aa_marks[("out", pc, pni)])
                            for tap in range(4):
                                last = t.matmul(
                                    pp[:, 0:tw],
                                    convd[:, c, tap, :],
                                    prebuf[:, slot, t0 + 1 + tap:
                                           t0 + 1 + tap + tw],
                                    start=(tap == 0), stop=(tap == 3))
                            last.then_inc(MM, 1)
                            cnt("MM", 1)
                        if y != "v":
                            sp = ps[0]
                            t.wait_ge(VV, vv_marks[("sq", c)])
                            for ni in range(4):
                                last = t.matmul(
                                    ps[ni][32 * 0:8, 0:512] if False else
                                    ps[ni][0:8, 0:512],
                                    hsel_s[:, (0 if y == "q" else 8) + c, :],
                                    sqb[:, ni * 512:(ni + 1) * 512],
                                    start=(c == 0), stop=(c == 7))
                                if c == 7:
                                    last.then_inc(MM, 1)
                                    cnt("MM", 1)
                        else:
                            t.wait_ge(FS[0], fs0[0] + 16 * (c + 1))
                            t.wait_ge(AA, aa_marks[("out", c, 4)])
                            f = 0
                            for name, ntaps in (("fs", 3), ("fl", 31)):
                                pp = ps[0] if name == "fs" else ps[1]
                                for ni, (t0, tw) in enumerate(NT_OWN):
                                    prev = None
                                    if name == "fs":
                                        prev = (c, f - 1) if f >= 1 else \
                                            ((c - 1, 3) if c > 0 else None)
                                    else:
                                        prev = (c, f - 1) if f >= 5 else \
                                            ((c - 1, 7) if c > 0 else None)
                                    if prev is not None:
                                        t.wait_ge(VV, vv_marks[
                                            ("fir", prev[0], prev[1])])
                                    base = t0 + 128 - (ntaps - 1)
                                    for tap in range(ntaps):
                                        dg = (fsd_s[:, c, tap, :]
                                              if name == "fs" else
                                              firdb[:, 0, tap, :])
                                        last = t.matmul(
                                            pp[:, 0:tw], dg,
                                            outb[:, slot, base + tap:
                                                 base + tap + tw],
                                            start=(tap == 0),
                                            stop=(tap == ntaps - 1))
                                    last.then_inc(MM, 1)
                                    cnt("MM", 1)
                                    f += 1
                    return ("MM", cnt("MM"))

                @S.section("scalar")
                def _(s):
                    last = None
                    for c in range(8):
                        slot = c % 2
                        for ni, (t0, tw) in enumerate(NT_EXT):
                            s.wait_ge(MM, mm_marks[("proj", c, ni)])
                            last = s.activation(
                                prebuf[:, slot, t0 + 4:t0 + 4 + tw],
                                ps[4][:, 0:tw], AF.Identity)
                            last.then_inc(AA, 1)
                            cnt("AA", 1)
                        for ni, (t0, tw) in enumerate(NT_EXT):
                            s.wait_ge(MM, mm_marks[("conv", c, ni)])
                            if c >= 2 and ni == 0 and y != "v":
                                pass
                            if c >= 2 and ni == 0:
                                s.wait_ge(OBS[slot], ob0[slot] +
                                          S.ob_per[y] * (c // 2))
                            last = s.activation(
                                outb[:, slot, t0:t0 + tw],
                                ps[5][:, 0:tw], AF.Silu)
                            last.then_inc(AA, 1)
                            cnt("AA", 1)
                    return ("AA", cnt("AA"))

                @S.section("vector")
                def _(v):
                    last = None
                    for c in range(8):
                        slot = c % 2
                        if y != "v":
                            v.wait_ge(AA, aa_marks[("out", c, 4)])
                            last = v.tensor_tensor(
                                sqb[:, :], outb[:, slot, 128:2176],
                                outb[:, slot, 128:2176], OP.mult)
                            last.then_inc(VV, 1)
                            cnt("VV", 1)
                        else:
                            f = 0
                            for name, ntaps in (("fs", 3), ("fl", 31)):
                                pp = ps[0] if name == "fs" else ps[1]
                                dstb = sqb if name == "fs" else lgb
                                for ni, (t0, tw) in enumerate(NT_OWN):
                                    v.wait_ge(MM, mm_marks[("fir", c, f)])
                                    if ni == 0 and c > 0:
                                        psl = (c - 1) % 2
                                        v.wait_ge(OBS[psl], ob0[psl] +
                                                  48 * ((c - 1) // 2 + 1))
                                    last = v.tensor_copy(
                                        dstb[:, t0:t0 + tw], pp[:, 0:tw])
                                    last.then_inc(VV, 1)
                                    cnt("VV", 1)
                                    f += 1
                    return ("VV", cnt("VV"))

                @S.section("sync")
                def _(sy):
                    last = None
                    for c in range(8):
                        slot = c % 2
                        nst = 0
                        if y == "q":
                            sy.wait_ge(AA, aa_marks[("out", c, 4)])
                            last = sy.dma_start(
                                out=qf_h.rearrange("(cc p) t -> p cc t",
                                                   p=128)[:, c, :],
                                in_=outb[:, slot, 128:2176])
                            last.then_inc(OBS[slot], 16)
                            cnt("OB" + "ab"[slot], 16)
                        elif y == "k":
                            sy.wait_ge(AA, aa_marks[("out", c, 4)])
                            last = sy.dma_start(
                                out=kf_h.rearrange("(cc p) t -> p cc t",
                                                   p=128)[:, c, :],
                                in_=outb[:, slot, 128:2176])
                            last.then_inc(OBS[slot], 16)
                            cnt("OB" + "ab"[slot], 16)
                        else:
                            sy.wait_ge(AA, aa_marks[("out", c, 4)])
                            last = sy.dma_start(
                                out=vf_h.rearrange("(cc p) t -> p cc t",
                                                   p=128)[:, c, :],
                                in_=outb[:, slot, 96:2176])
                            last.then_inc(OBS[slot], 16)
                            cnt("OB" + "ab"[slot], 16)
                            v_last = vv_marks[("fir", c, 7)]
                            sy.wait_ge(VV, v_last)
                            last = sy.dma_start(
                                out=sh_h.rearrange("(cc p) t -> p cc t",
                                                   p=128)[:, c, :],
                                in_=sqb[:, :])
                            last.then_inc(OBS[slot], 16)
                            cnt("OB" + "ab"[slot], 16)
                            last = sy.dma_start(
                                out=lg_h.rearrange("(cc p) t -> p cc t",
                                                   p=128)[:, c, :],
                                in_=lgb[:, :])
                            last.then_inc(OBS[slot], 16)
                            cnt("OB" + "ab"[slot], 16)
                    sl = 7 % 2
                    sy.wait_ge(OBS[1 - sl], cnt("OB" + "ab"[1 - sl]))
                    return ("OB" + "ab"[sl], cnt("OB" + "ab"[sl]))

            S.ob_per = {"q": 16, "k": 16, "v": 48}
            subphase_proj("q")
            S.barrier()
            @S.section("vector")
            def _(v):
                last = None
                for ni in range(4):
                    last = v.tensor_copy(stats[0:8, ts(ni, 512)],
                                         ps[ni][0:8, 0:512])
                    last.then_inc(VV, 1)
                    cnt("VV", 1)
                return ("VV", cnt("VV"))

            @S.section("sync")
            def _(sy):
                sy.wait_ge(VV, cnt("VV"))
                last = sy.dma_start(out=sp_h[0:4, :], in_=stats[0:4, :])
                last.then_inc(STO, 16)
                cnt("STO", 16)
                return ("STO", cnt("STO"))

            S.barrier()
            subphase_proj("k")
            S.barrier()

            @S.section("vector")
            def _(v):
                last = None
                for ni in range(4):
                    last = v.tensor_copy(stats[0:8, ts(ni, 512)],
                                         ps[ni][0:8, 0:512])
                    last.then_inc(VV, 1)
                    cnt("VV", 1)
                return ("VV", cnt("VV"))

            @S.section("sync")
            def _(sy):
                sy.wait_ge(VV, cnt("VV"))
                last = sy.dma_start(out=sp_h[4:8, :], in_=stats[4:8, :])
                last.then_inc(STO, 16)
                cnt("STO", 16)
                return ("STO", cnt("STO"))

            S.barrier()
            subphase_proj("v")
            S.barrier()

            S.aa_beta0 = cnt("AA")
            S.mm_beta0 = cnt("MM")

            @S.section("tensor")
            def _(t):
                last = None
                for i in range(NTOWN):
                    pp = ps[2 + i % 2]
                    if i >= 2:
                        t.wait_ge(AA, S.aa_beta0 + (i - 2) + 1)
                    for kc in range(8):
                        last = t.matmul(pp[:, 0:H], xn[:, kc, ts(i + 1)],
                                        wb_s[:, kc, :],
                                        start=(kc == 0), stop=(kc == 7))
                    last.then_inc(MM, 1)
                    cnt("MM", 1)
                return ("MM", cnt("MM"))

            @S.section("scalar")
            def _(s):
                last = None
                for i in range(NTOWN):
                    s.wait_ge(MM, S.mm_beta0 + i + 1)
                    last = s.activation(beta_tm[:, i, :],
                                        ps[2 + i % 2][:, 0:H], AF.Sigmoid)
                    last.then_inc(AA, 1)
                    cnt("AA", 1)
                return ("AA", cnt("AA"))

            S.barrier()

            stats_tm = pool[:, 6144:6272].rearrange(
                "p (t r) -> p t r", t=16)
            stats32 = pool[0:32, 2048:4096]
            stt2 = pool[0:32, 4096:6144]

            @S.section("gpsimd")
            def _(g):
                g.memset(stats32[:, :], 0.0).then_inc(GG, 1)
                cnt("GG", 1)
                g.wait_ge(GG, cnt("GG"))
                last = g.dma_start(out=stats32[0:8, :], in_=sp_h[:, :])
                last.then_inc(LG, 16)
                cnt("LG", 16)
                g.wait_ge(VV, cnt("VV") + 1)
                s2v = stt2.rearrange("i (t c) -> i t c", t=16)
                for b_ in range(4):
                    last = g.dma_start(
                        out=stats_tm[32 * b_:32 * b_ + 32, :, :],
                        in_=s2v[:, :, 32 * b_:32 * b_ + 8])
                    last.then_inc(LG, 16)
                    cnt("LG", 16)
                return ("LG", cnt("LG"))

            @S.section("vector")
            def _(v):
                v.wait_ge(LG, cnt("LG") - 64)
                v.tensor_scalar(stats32[0:8, :], stats32[0:8, :], 1e-12,
                                None, OP.add)
                v.drain()
                last = v.transpose(stt2[:, :], stats32[:, :])
                last.then_inc(VV, 1)
                cnt("VV", 1)
                return ("VV", cnt("VV"))

            @S.section("scalar")
            def _(s):
                s.wait_ge(LG, cnt("LG"))
                s.activation(ppk[:, :, :], stats_tm[:, :, 0:4], AF.Ln)
                s.activation(spk[:, :, :], stats_tm[:, :, 4:8], AF.Ln)
                s.drain()
                s.activation(ppk[:, :, :], ppk[:, :, :], AF.Exp, scale=-0.5)
                last = s.activation(spk[:, :, :], spk[:, :, :], AF.Exp,
                                    scale=-0.5)
                last.then_inc(AA, 1)
                cnt("AA", 1)
                return ("AA", cnt("AA"))

            @S.section("vector")
            def _(v):
                v.wait_ge(AA, cnt("AA"))
                v.tensor_tensor(bspk[:, :, :], beta_tm[:, :, :],
                                spk[:, :, :], OP.mult)
                last = v.tensor_scalar(nspk[:, :, :], spk[:, :, :], -1.0,
                                       None, OP.mult)
                last.then_inc(VV, 1)
                cnt("VV", 1)
                return ("VV", cnt("VV"))

            S.barrier()

            fgxbuf = pool[:, 0:1024]
            fgxbuf16 = pool[:, 1024:1536].bitcast(BF16)

            S.fgx_vv0 = cnt("VV")
            S.fgx_sto0 = cnt("STO")

            @S.section("gpsimd")
            def _(g):
                last = g.dma_start(out=fgx_h[0:D, :], in_=fgw1T[0:D, :])
                last.then_inc(LG, 16)
                cnt("LG", 16)
                for i in range(6):
                    blk, kc = divmod(i, 2)
                    r0 = D + blk * 256 + kc * 128
                    if i > 0:
                        g.wait_ge(VV, S.fgx_vv0 + i)
                    last = g.dma_start(out=fgxbuf[:, :],
                                       in_=fgw1T[r0:r0 + 128, :])
                    last.then_inc(LG, 16)
                    cnt("LG", 16)
                return ("LG", cnt("LG"))

            @S.section("vector")
            def _(v):
                last = None
                lgc = cnt("LG") - 16 * 6
                for i in range(6):
                    v.wait_ge(LG, lgc + 16 * (i + 1))
                    if i > 0:
                        v.wait_ge(STO, S.fgx_sto0 + 64 * i)
                    last = v.tensor_scalar(fgxbuf16[:, :], fgxbuf[:, :],
                                           0.25, None, OP.mult)
                    last.then_inc(VV, 1)
                    cnt("VV", 1)
                return ("VV", cnt("VV"))

            @S.section("sync")
            def _(sy):
                last = None
                vvc = cnt("VV") - 6
                for i in range(6):
                    blk, kc = divmod(i, 2)
                    sy.wait_ge(VV, vvc + i + 1)
                    for rep in range(4):
                        r0 = D + blk * 1024 + rep * 256 + kc * 128
                        last = sy.dma_start(out=fgx_h[r0:r0 + 128, :],
                                            in_=fgxbuf16[:, :])
                        last.then_inc(STO, 16)
                        cnt("STO", 16)
                return ("STO", cnt("STO"))

            S.barrier()
            if "qkv" in dbg:
                @S.section("gpsimd")
                def _(g):
                    last = None
                    for dst, srcb in [("q", qf_h), ("k", kf_h), ("v", vf_h),
                                      ("short", sh_h), ("long", lg_h)]:
                        last = g.dma_start(out=dbg_aps[dst], in_=srcb[:, :])
                        last.then_inc(LG, 16)
                        cnt("LG", 16)
                    last = g.dma_start(out=dbg_aps["stats"], in_=sp_h[:, :])
                    last.then_inc(LG, 16)
                    cnt("LG", 16)
                    return ("LG", cnt("LG"))

                @S.section("scalar")
                def _(s):
                    s.drain()
                    last = s.dma_start(out=dbg_aps["beta"],
                                       in_=beta_tm[:, :, :])
                    last.then_inc(STO, 16)
                    cnt("STO", 16)
                    return ("STO", cnt("STO"))
                S.barrier()

        if "DELTA" in phases:
            Xs = pool[:, 0:4096].rearrange("p (s f) -> p s f", s=8)
            Xs16 = pool[:, 14848:16896].bitcast(BF16).rearrange(
                "p (s f) -> p s f", s=8)
            c2b = pool[:, 4096:4608].bitcast(BF16).rearrange(
                "p (s f) -> p s f", s=2)
            ybuf = pool[:, 5120:6144].bitcast(BF16).rearrange(
                "p (g h f) -> p g h f", g=2, h=2)
            gbuf = pool[:, 6144:7168].bitcast(BF16).rearrange(
                "p (jg h f) -> p jg h f", jg=2, h=4)
            qkbuf = pool[:, 7168:8192].bitcast(BF16).rearrange(
                "p (jg h f) -> p jg h f", jg=2, h=4)
            tbt = pool[:, 8192:9216].bitcast(BF16).rearrange(
                "p (g f) -> p g f", g=16)
            twB = pool[:, 9216:9280].bitcast(BF16)
            twBT = pool[:, 9280:9344].bitcast(BF16)
            twR = pool[:, 9344:9408].bitcast(BF16)
            twS = pool[:, 9408:9728]
            pool2 = arena[:, 8704:14848]
            kfbs = [pool[:, 9728:10752].bitcast(BF16).rearrange(
                "p (s f) -> p s f", s=8),
                pool2[:, 0:1024].bitcast(BF16).rearrange(
                "p (s f) -> p s f", s=8)]
            qfbs = [pool[:, 10752:11776].bitcast(BF16).rearrange(
                "p (s f) -> p s f", s=8),
                pool2[:, 1024:2048].bitcast(BF16).rearrange(
                "p (s f) -> p s f", s=8)]
            ktms = [pool[:, 11776:12800].bitcast(BF16).rearrange(
                "p (s f) -> p s f", s=2),
                pool2[:, 2048:3072].bitcast(BF16).rearrange(
                "p (s f) -> p s f", s=2)]
            vtms = [pool[:, 12800:13824].bitcast(BF16).rearrange(
                "p (s f) -> p s f", s=2),
                pool2[:, 3072:4096].bitcast(BF16).rearrange(
                "p (s f) -> p s f", s=2)]
            obufs = [pool[:, 13824:14848].bitcast(BF16).rearrange(
                "p (g f) -> p g f", g=2),
                pool2[:, 4096:5120].bitcast(BF16).rearrange(
                "p (g f) -> p g f", g=2)]
            sgm = pool[:, 9728:11776].rearrange("p (s f) -> p s f", s=8)
            clb = pool[:, 11776:13824].rearrange("p (s f) -> p s f", s=8)
            ohb = sb("ohb", [128, 8], F32)
            trius_s = sb("trius_s", [128, 32], BF16)
            eye4_s = sb("eye4_s", [128, 32], BF16)
            triuqk_s = sb("triuqk_s", [128, 32], BF16)
            trils_s = sb("trils_s", [128, 32], BF16)
            cmask_s = pool[:, 16896:17920].bitcast(BF16).rearrange(
                "p (q f) -> p q f", q=4)
            zb16 = pool[:, 4608:4864].bitcast(BF16)
            twG4 = pool2[:, 5120:5376].bitcast(BF16).rearrange(
                "p (h f) -> p h f", h=4)
            twQ4 = pool2[:, 5376:5632].bitcast(BF16).rearrange(
                "p (h f) -> p h f", h=4)
            twP4 = pool2[:, 5632:5888].bitcast(BF16).rearrange(
                "p (h f) -> p h f", h=4)
            twR4 = pool2[:, 5888:6144].bitcast(BF16).rearrange(
                "p (h f) -> p h f", h=4)
            twB4 = twG4
            twQf = pool2[:, 5376:5632].bitcast(BF16)
            twPf = pool2[:, 5632:5888].bitcast(BF16)
            twRf = pool2[:, 5888:6144].bitcast(BF16)
            DS = [S.sem("DSa"), S.sem("DSb")]
            OBS2 = [S.sem("OB2a"), S.sem("OB2b")]

            OHS = S.sem("OHS")

            @S.section("gpsimd")
            def _(g):
                g.dma_start(out=ohb[:, :], in_=onehot[:, :]).then_inc(OHS, 16)
                g.memset(pool[:, 4608:4864], 0.0).then_inc(GG, 1)
                cnt("GG", 1)
                last = g.memset(Xs[:, :, :], 0.0)
                last.then_inc(GG, 1)
                cnt("GG", 1)
                g.wait_ge(OHS, 16)
                return ("GG", cnt("GG"))

            @S.section("scalar")
            def _(sg):
                last = sg.dma_start(out=trius_s[:], in_=trius[:]) \
                    if False else None
                return None

            @S.section("gpsimd")
            def _(g2e):
                last = g2e.dma_start(out=trius_s[:], in_=trius[:])
                last.then_inc(LG, 16)
                cnt("LG", 16)
                last = g2e.dma_start(out=eye4_s[:], in_=eye4[:])
                last.then_inc(LG, 16)
                cnt("LG", 16)
                last = g2e.dma_start(out=triuqk_s[:], in_=triuqk[:])
                last.then_inc(LG, 16)
                cnt("LG", 16)
                last = g2e.dma_start(out=trils_s[:], in_=trils[:])
                last.then_inc(LG, 16)
                cnt("LG", 16)
                last = g2e.dma_start(out=cmask_s[:, :, :], in_=cmask[:, :])
                last.then_inc(LG, 16)
                cnt("LG", 16)
                return ("LG", cnt("LG"))

            @S.section("vector")
            def _(v):
                v.wait_ge(GG, cnt("GG"))
                v.wait_ge(LG, cnt("LG"))
                for sl in range(8):
                    kc = sl % 2
                    v.tensor_copy(Xs[:, sl, 256 + 128 * kc:384 + 128 * kc],
                                  idf[:, :])
                v.drain()
                last = None
                for sl in range(8):
                    last = v.tensor_copy(Xs16[:, sl, :], Xs[:, sl, :])
                last.then_inc(VV, 1)
                cnt("VV", 1)
                return ("VV", cnt("VV"))

            S.barrier()

            def delta_pass(mode):
                W = 512 if mode == "p1" else 256
                TI = 5 if mode == "p1" else 7
                VI = 2 if mode == "p1" else 3
                PM = {}
                PV = {}
                if mode == "p1":
                    _m, _v = 8, 4
                    for g2_ in range(2):
                        for h_ in range(4):
                            _m += 1
                            PM[("gd", g2_, h_)] = _m
                        for h_ in range(4):
                            _m += 1
                            PM[("t1", g2_, h_)] = _m
                        for h_ in range(4):
                            _m += 1
                            PM[("t2", g2_, h_)] = _m
                        for k_ in range(1, 5):
                            for nm_ in ("P", "Q", "F"):
                                _m += 1
                                PM[(nm_, g2_, k_)] = _m
                        for h_ in range(4):
                            _v += 1
                            PV[("gts", g2_, h_)] = _v
                        for h_ in range(4):
                            _v += 1
                            PV[("q", g2_, h_)] = _v
                        for h_ in range(4):
                            _v += 1
                            PV[("b", g2_, h_)] = _v
                        _v += 1
                        PV[("r0", g2_)] = _v
                        for k_ in range(1, 5):
                            for nm_ in ("pe", "qe", "re"):
                                _v += 1
                                PV[(nm_, g2_, k_)] = _v
                        _v += 1
                        PV[("fin", g2_)] = _v
                    TP, VP = _m, _v
                else:
                    TP, VP = 16, 8
                VE = 12
                TSC = TP + 2 * 8 * TI
                VSC = VP + 2 * 8 * VI + VE
                mm0 = cnt("MM")
                vv0 = cnt("VV")
                ds0 = [cnt("DSa"), cnt("DSb")]
                ob0 = [cnt("OB2a"), cnt("OB2b")]
                NL = 5 if mode == "p1" else 6

                def mmv(sc, x):
                    return mm0 + TSC * sc + x

                def vvv(sc, x):
                    return vv0 + VSC * sc + x

                def smm(sc, pair, i, x):
                    return mmv(sc, TP + (pair * 8 + i) * TI + x)

                def svv(sc, pair, i, x):
                    return vvv(sc, VP + pair * (8 * VI + 4) + i * VI + x)

                @S.section("sync")
                def _(sy):
                    last = None
                    for sc in range(DNSC):
                        sl = sc % 2
                        if sc >= 2:
                            sy.wait_ge(MM, mmv(sc - 1, 0))
                            sy.wait_ge(VV, vvv(sc - 1, 0))
                        t0 = sc * SCK
                        if "nold" in P1X:
                            last = sy.dma_start(out=kfbs[sl][:, 0, 0:2],
                                                in_=kf_h[0:128, 0:2])
                        else:
                            last = sy.dma_start(
                                out=kfbs[sl][:, :, :],
                                in_=kf_h.rearrange("(c p) t -> p c t", p=128)
                                [:, :, t0:t0 + SCK])
                        last.then_inc(DS[sl], 16)
                        cnt("DS" + "ab"[sl], 16)
                        if mode == "p2":
                            last = sy.dma_start(
                                out=qfbs[sl][:, :, :],
                                in_=qf_h.rearrange("(c p) t -> p c t", p=128)
                                [:, :, t0:t0 + SCK])
                            last.then_inc(DS[sl], 16)
                            cnt("DS" + "ab"[sl], 16)
                        for half in range(2):
                            tt = t0 + 128 * half
                            if "nold" in P1X:
                                last = sy.dma_start(
                                    out=ktms[sl][:, half, 0:2],
                                    in_=kf_h[0:128, 0:2])
                            else:
                                last = sy.dma_start(
                                    out=ktms[sl][:, half, :],
                                    in_=kf_h[:, tt:tt + 128], transpose=True)
                            last.then_inc(DS[sl], 16)
                            cnt("DS" + "ab"[sl], 16)
                            if "nold" in P1X:
                                last = sy.dma_start(
                                    out=vtms[sl][:, half, 0:2],
                                    in_=vf_h[0:128, 0:2])
                            else:
                                last = sy.dma_start(
                                    out=vtms[sl][:, half, :],
                                    in_=vf_h[:, tt + 32:tt + 160],
                                    transpose=True)
                            last.then_inc(DS[sl], 16)
                            cnt("DS" + "ab"[sl], 16)
                        if mode == "p2":
                            if sc > 0:
                                for gg2 in range(2):
                                    sy.wait_ge(VV, svv(sc - 1, 1, 7, 3))
                                    last = sy.dma_start(
                                        out=dl_tm_h[(sc - 1) * SCK +
                                                    128 * gg2:
                                                    (sc - 1) * SCK +
                                                    128 * gg2 + 128, :],
                                        in_=obufs[(sc - 1) % 2][:, gg2, :])
                                    last.then_inc(OBS2[(sc - 1) % 2], 16)
                                    cnt("OB2" + "ab"[(sc - 1) % 2], 16)
                    if mode == "p2":
                        for gg2 in range(2):
                            sy.wait_ge(VV, svv(DNSC - 1, 1, 7, 3))
                            last = sy.dma_start(
                                out=dl_tm_h[(DNSC - 1) * SCK + 128 * gg2:
                                            (DNSC - 1) * SCK +
                                            128 * gg2 + 128, :],
                                in_=obufs[(DNSC - 1) % 2][:, gg2, :])
                            last.then_inc(OBS2[(DNSC - 1) % 2], 16)
                            cnt("OB2" + "ab"[(DNSC - 1) % 2], 16)
                        sy.wait_ge(OB2 := OBS2[0], cnt("OB2a"))
                        sy.wait_ge(OBS2[1], cnt("OB2b"))
                    sl = (DNSC - 1) % 2
                    sy.wait_ge(DS[1 - sl], cnt("DS" + "ab"[1 - sl]))
                    return ("DS" + "ab"[sl], cnt("DS" + "ab"[sl]))

                @S.section("tensor")
                def _(t):
                    last = None
                    for sc in range(DNSC):
                        sl = sc % 2
                        t.wait_ge(DS[sl], ds0[sl] + 16 * NL * (sc // 2 + 1))
                        if sc > 0:
                            t.wait_ge(VV, vvv(sc - 1, VSC))
                        x = 0
                        for jg in range(2):
                            if jg == 1:
                                t.wait_ge(VV, vvv(sc, 2))
                            for js in range(4):
                                j = 4 * jg + js
                                for ph in range(2):
                                    pp = ps[0 + ph]
                                    for h2 in range(2):
                                        h = 2 * ph + h2
                                        for kc in range(2):
                                            last = t.matmul(
                                                pp[32 * js:32 * js + 32,
                                                   256 * h2:256 * h2 + 256],
                                                kfbs[sl][:, 2 * h + kc,
                                                         32 * j:32 * j + 32],
                                                kfbs[sl][:, 2 * h + kc, :],
                                                start=(kc == 0),
                                                stop=(kc == 1),
                                                tile_position=(0, 32 * js))
                                last.then_inc(MM, 1)
                                cnt("MM", 1)
                                x += 1
                        if mode == "p2":
                            for jg in range(2):
                                if jg == 1:
                                    t.wait_ge(VV, vvv(sc, 6))
                                for js in range(4):
                                    j = 4 * jg + js
                                    for ph in range(2):
                                        pp = ps[2 + ph]
                                        for h2 in range(2):
                                            h = 2 * ph + h2
                                            for kc in range(2):
                                                last = t.matmul(
                                                    pp[32 * js:32 * js + 32,
                                                       256 * h2:
                                                       256 * h2 + 256],
                                                    kfbs[sl][:,
                                                             2 * h + kc,
                                                             32 * j:
                                                             32 * j + 32],
                                                    qfbs[sl][:, 2 * h + kc,
                                                             :],
                                                    start=(kc == 0),
                                                    stop=(kc == 1),
                                                    tile_position=(0,
                                                                   32 * js))
                                    last.then_inc(MM, 1)
                                    cnt("MM", 1)
                        else:
                            for g2 in range(2):
                                ti4 = 2 * sc + g2
                                if g2 == 1:
                                    t.wait_ge(VV, vvv(sc, PV[("re", 0, 4)]))
                                t.matmul(ps[6][:, 0:512], id16[:, :],
                                         zb16[:, :], start=True, stop=True)
                                for h in range(4):
                                    for cs in range(4):
                                        i = 4 * g2 + cs
                                        for kc in range(2):
                                            last = t.matmul(
                                                ps[6][32 * cs:32 * cs + 32,
                                                      128 * h + 32 * cs:
                                                      128 * h + 32 * cs + 32],
                                                kfbs[sl][:, 2 * h + kc,
                                                         32 * i:32 * i + 32],
                                                kfbs[sl][:, 2 * h + kc,
                                                         32 * i:32 * i + 32],
                                                start=(kc == 0),
                                                stop=(kc == 1),
                                                tile_position=(0, 32 * cs))
                                    last.then_inc(MM, 1)
                                    cnt("MM", 1)
                                for h in range(4):
                                    if h >= 1:
                                        t.wait_ge(VV, vvv(
                                            sc, PV[("q", g2, h - 1)]))
                                    elif g2 == 1:
                                        t.wait_ge(VV, vvv(
                                            sc, PV[("b", 0, 3)]))
                                    t.wait_ge(VV, vvv(
                                        sc, PV[("gts", g2, h)]))
                                    if "tr" in P1X:
                                        last = t.transpose(
                                            pt16[0][:, 0:128],
                                            id16[:], id16[:])
                                    else:
                                        last = t.transpose(
                                            pt16[0][:, 0:128],
                                            twG4[:, h, :], id16[:])
                                    last.then_inc(MM, 1)
                                    cnt("MM", 1)
                                for h in range(4):
                                    t.wait_ge(VV, vvv(
                                        sc, PV[("q", g2, 3)] if h == 0
                                        else PV[("b", g2, h - 1)]))
                                    if "tr" in P1X:
                                        last = t.transpose(
                                            pt16[0][:, 0:128],
                                            id16[:], id16[:])
                                    else:
                                        last = t.transpose(
                                            pt16[0][:, 0:128],
                                            twQ4[:, h, :], id16[:])
                                    last.then_inc(MM, 1)
                                    cnt("MM", 1)
                                for k in range(1, 5):
                                    rhsP = twB4 if k == 1 else twP4
                                    t.wait_ge(VV, vvv(
                                        sc, PV[("b", g2, 3)] if k == 1
                                        else PV[("qe", g2, k - 1)]))
                                    for h in range(4):
                                        last = t.matmul(
                                            ps[4][:, 128 * h:128 * h + 128],
                                            twQ4[:, h, :], rhsP[:, h, :],
                                            start=True, stop=True)
                                    last.then_inc(MM, 1)
                                    cnt("MM", 1)
                                    for h in range(4):
                                        last = t.matmul(
                                            ps[5][:, 128 * h:128 * h + 128],
                                            rhsP[:, h, :], twQ4[:, h, :],
                                            start=True, stop=True)
                                    last.then_inc(MM, 1)
                                    cnt("MM", 1)
                                    t.wait_ge(VV, vvv(sc, PV[("qe", g2, k)]))
                                    if k == 1:
                                        t.wait_ge(VV, vvv(
                                            sc, PV[("r0", g2)]))
                                    for h in range(4):
                                        last = t.matmul(
                                            ps[6][:, 128 * h:128 * h + 128],
                                            twQ4[:, h, :], twR4[:, h, :],
                                            start=True, stop=True)
                                    last.then_inc(MM, 1)
                                    cnt("MM", 1)
                        if "noch" in P1X:
                            for _k in range(2 * 8 * TI):
                                last = t.matmul(ps[0][0:32, 0:32],
                                                id16[:, 0:32],
                                                id16[:, 0:32],
                                                start=True, stop=True)
                                last.then_inc(MM, 1)
                                cnt("MM", 1)
                        _CH = "noch" not in P1X
                        for pair in (range(2) if _CH else []):
                            for i in range(8):
                                gi2 = i // 4
                                stp = 32 * (i % 4)
                                if (pair, i) == (0, 0):
                                    t.wait_ge(VV, vvv(sc, VP))
                                for h2 in range(2):
                                    h = 2 * pair + h2
                                    pp = ps[0 + h2]
                                    if (pair * 8 + i) > 0:
                                        pi = pair * 8 + i - 1
                                        t.wait_ge(VV, svv(sc, pi // 8,
                                                          pi % 8, 2))
                                    jgs = list(range(i // 4)) + (
                                        [i // 4] if i % 4 else [])
                                    nmm = 2 + len(jgs)
                                    cnt_m = 0
                                    for kc in range(2):
                                        cnt_m += 1
                                        last = t.matmul(
                                            pp[stp:stp + 32, 0:W],
                                            kfbs[sl][:, 2 * h + kc,
                                                     32 * i:32 * i + 32],
                                            Xs16[:, 2 * h + kc, 0:W],
                                            start=(cnt_m == 1),
                                            stop=(cnt_m == nmm),
                                            tile_position=(0, stp))
                                    for jg in jgs:
                                        cnt_m += 1
                                        last = t.matmul(
                                            pp[stp:stp + 32, 0:W],
                                            gbuf[:, jg, h,
                                                 32 * i:32 * i + 32],
                                            ybuf[:, jg, h2, 0:W],
                                            start=False,
                                            stop=(cnt_m == nmm),
                                            tile_position=(0, stp))
                                    last.then_inc(MM, 1)
                                    cnt("MM", 1)
                                t.wait_ge(VV, svv(sc, pair, i, 1))
                                for h2 in range(2):
                                    last = t.matmul(
                                        ps[0 + h2][stp:stp + 32, 0:W],
                                        tbt[:, 2 * sc + gi2, :]
                                        [stp:stp + 32,
                                         32 * (2 * pair + h2):
                                         32 * (2 * pair + h2) + 32],
                                        c2b[:, h2, 0:W][stp:stp + 32, :],
                                        start=True, stop=True,
                                        tile_position=(stp, stp))
                                    last.then_inc(MM, 1)
                                    cnt("MM", 1)
                                t.wait_ge(VV, svv(sc, pair, i, 2))
                                if pair == 1 and i == 0:
                                    t.wait_ge(VV, vvv(sc, VP + 8 * VI + 4))
                                if i % 4 == 3:
                                    for h2 in range(2):
                                        h = 2 * pair + h2
                                        for mc in range(2):
                                            last = t.matmul(
                                                ps[2 + 2 * h2 + mc][:, 0:W],
                                                ktms[sl][:, gi2,
                                                         256 * h + 128 * mc:
                                                         256 * h +
                                                         128 * mc + 128],
                                                ybuf[:, gi2, h2, 0:W],
                                                start=(i == 3),
                                                stop=(i == 7))
                                else:
                                    last = t.matmul(ps[0][0:32, 0:32],
                                                    id16[:, 0:32],
                                                    id16[:, 0:32],
                                                    start=True, stop=True)
                                last.then_inc(MM, 1)
                                cnt("MM", 1)
                                if mode == "p2":
                                    if pair * 8 + i >= 4:
                                        pi = pair * 8 + i - 4
                                        t.wait_ge(VV, svv(sc, pi // 8,
                                                          pi % 8, 3))
                                    for h2 in range(2):
                                        h = 2 * pair + h2
                                        njg = i // 4 + 1
                                        nmm = 2 + njg
                                        cnt_m = 0
                                        for kc in range(2):
                                            cnt_m += 1
                                            last = t.matmul(
                                                ps[6][stp:stp + 32,
                                                      256 * h2:
                                                      256 * h2 + 256],
                                                qfbs[sl][:, 2 * h + kc,
                                                         32 * i:32 * i + 32],
                                                Xs16[:, 2 * h + kc, 0:256],
                                                start=(cnt_m == 1),
                                                stop=(cnt_m == nmm),
                                                tile_position=(0, stp))
                                        for jg in range(njg):
                                            cnt_m += 1
                                            last = t.matmul(
                                                ps[6][stp:stp + 32,
                                                      256 * h2:
                                                      256 * h2 + 256],
                                                qkbuf[:, jg, h,
                                                      32 * i:32 * i + 32],
                                                ybuf[:, jg, h2, 0:256],
                                                start=False,
                                                stop=(cnt_m == nmm),
                                                tile_position=(0, stp))
                                        last.then_inc(MM, 1)
                                        cnt("MM", 1)
                    assert cnt("MM") - mm0 == TSC * DNSC, (
                        mode, cnt("MM") - mm0, TSC * DNSC)
                    return ("MM", cnt("MM"))

                @S.section("vector")
                def _(v):
                    last = None
                    for sc in range(DNSC):
                        sl = sc % 2
                        x = 0
                        for jg in range(2):
                            v.wait_ge(MM, mmv(sc, 4 * (jg + 1)))
                            for ph in range(2):
                                last = v.tensor_tensor(
                                    gbuf[:, jg, 2 * ph:2 * ph + 2, :]
                                    .rearrange("p h f -> p (h f)"),
                                    ps[0 + ph][:, 0:512],
                                    cmask_s[:, jg, :], OP.mult)
                                last.then_inc(VV, 1)
                                cnt("VV", 1)
                                x += 1
                        if mode == "p2":
                            for jg in range(2):
                                v.wait_ge(MM, mmv(sc, 8 + 4 * (jg + 1)))
                                for ph in range(2):
                                    last = v.tensor_tensor(
                                        qkbuf[:, jg, 2 * ph:2 * ph + 2, :]
                                        .rearrange("p h f -> p (h f)"),
                                        ps[2 + ph][:, 0:512],
                                        cmask_s[:, 2 + jg, :], OP.mult)
                                    last.then_inc(VV, 1)
                                    cnt("VV", 1)
                        else:
                            for g2 in range(2):
                                ti4 = 2 * sc + g2
                                for h in range(4):
                                    v.wait_ge(MM, mmv(
                                        sc, PM[("gd", g2, h)]))
                                    last = v.tensor_scalar(
                                        twG4[:, h, :],
                                        ps[6][:, 128 * h:128 * h + 128],
                                        nspk[:, ti4, h:h + 1], None,
                                        OP.mult)
                                    last.then_inc(VV, 1)
                                    cnt("VV", 1)
                                for h in range(4):
                                    v.wait_ge(MM, mmv(
                                        sc, PM[("t1", g2, h)]))
                                    v.tensor_scalar(
                                        twQ4[:, h, :], pt16[0][:, 0:128],
                                        bspk[:, ti4, h:h + 1], None,
                                        OP.mult)
                                    v.drain()
                                    if "mask" in P1X:
                                        last = v.tensor_scalar(
                                            twQ4[:, h, :], twQ4[:, h, :],
                                            1.0, None, OP.mult)
                                    else:
                                        qv = twQ4[:, h, :].rearrange(
                                            "p (b f) -> p b f", b=4)
                                        a0, b0 = bass.broadcast_tensor_aps(
                                            qv, trils_s[:, :].rearrange(
                                                "p (o f) -> p o f", o=1))
                                        last = v.tensor_tensor(qv, a0, b0,
                                                               OP.mult)
                                    last.then_inc(VV, 1)
                                    cnt("VV", 1)
                                for h in range(4):
                                    v.wait_ge(MM, mmv(
                                        sc, PM[("t2", g2, h)]))
                                    last = v.tensor_copy(twB4[:, h, :],
                                                         pt16[0][:, 0:128])
                                    last.then_inc(VV, 1)
                                    cnt("VV", 1)
                                v.drain()
                                for h in range(4):
                                    last = v.tensor_tensor(
                                        twR4[:, h, :], twB4[:, h, :],
                                        id16[:, :], OP.add)
                                last.then_inc(VV, 1)
                                cnt("VV", 1)
                                for k in range(1, 5):
                                    v.wait_ge(MM, mmv(sc, PM[("Q", g2, k)]))
                                    last = v.tensor_copy(twPf[:, :],
                                                         ps[4][:, 0:512])
                                    last.then_inc(VV, 1)
                                    cnt("VV", 1)
                                    last = v.tensor_copy(twQf[:, :],
                                                         ps[5][:, 0:512])
                                    last.then_inc(VV, 1)
                                    cnt("VV", 1)
                                    v.wait_ge(MM, mmv(sc, PM[("F", g2, k)]))
                                    v.drain()
                                    last = v.tensor_tensor(
                                        twRf[:, :], ps[6][:, 0:512],
                                        twRf[:, :], OP.add)
                                    last.then_inc(VV, 1)
                                    cnt("VV", 1)
                                v.drain()
                                for h in range(4):
                                    for cs in range(4):
                                        last = v.tensor_scalar(
                                            tbt[:, ti4, :]
                                            [32 * cs:32 * cs + 32,
                                             32 * h:32 * h + 32],
                                            twR4[:, h, :]
                                            [32 * cs:32 * cs + 32,
                                             32 * cs:32 * cs + 32],
                                            beta_tm[:, ti4, h:h + 1]
                                            [32 * cs:32 * cs + 32, :],
                                            None, OP.mult)
                                last.then_inc(VV, 1)
                                cnt("VV", 1)
                        assert cnt("VV") - vv0 - VSC * sc <= VP, (
                            mode, sc, cnt("VV") - vv0 - VSC * sc, VP)
                        while cnt("VV") - vv0 - VSC * sc < VP:
                            v.drain()
                            last = v.tensor_copy(twS[:, 0:1], twS[:, 0:1])
                            last.then_inc(VV, 1)
                            cnt("VV", 1)
                        if "noch" in P1X:
                            for _k in range(2 * (8 * VI + 4)):
                                v.drain()
                                last = v.tensor_copy(twS[:, 0:1],
                                                     twS[:, 0:1])
                                last.then_inc(VV, 1)
                                cnt("VV", 1)
                        for pair in (range(2) if "noch" not in P1X
                                     else []):
                            if pair == 1:
                                v.wait_ge(MM, smm(sc, 0, 7, 5))
                                for h2 in range(2):
                                    for mc in range(2):
                                        last = v.tensor_tensor(
                                            Xs[:, 2 * h2 + mc, 0:W],
                                            ps[2 + 2 * h2 + mc][:, 0:W],
                                            Xs[:, 2 * h2 + mc, 0:W], OP.add)
                                        last.then_inc(VV, 1)
                                        cnt("VV", 1)
                            for i in range(8):
                                gi2 = i // 4
                                stp = 32 * (i % 4)
                                for h2 in range(2):
                                    h = 2 * pair + h2
                                    v.wait_ge(MM, smm(sc, pair, i, 1 + h2))
                                    if mode == "p1":
                                        if "stt" in P1X:
                                            v.tensor_scalar(
                                                c2b[:, h2, 0:256]
                                                [stp:stp + 32, :],
                                                ps[0 + h2]
                                                [stp:stp + 32, 0:256],
                                                nspk[:, 2 * sc + gi2,
                                                     h:h + 1]
                                                [stp:stp + 32, :], None,
                                                OP.mult)
                                        else:
                                            v.scalar_tensor_tensor(
                                                c2b[:, h2, 0:256]
                                                [stp:stp + 32, :],
                                                ps[0 + h2]
                                                [stp:stp + 32, 0:256],
                                                nspk[:, 2 * sc + gi2,
                                                     h:h + 1]
                                                [stp:stp + 32, :],
                                                vtms[sl][:, gi2, :]
                                                [stp:stp + 32,
                                                 256 * h:256 * h + 256],
                                                OP.mult, OP.add)
                                        v.tensor_scalar(
                                            c2b[:, h2, 256:512]
                                            [stp:stp + 32, :],
                                            ps[0 + h2][stp:stp + 32,
                                                       256:512],
                                            nspk[:, 2 * sc + gi2, h:h + 1]
                                            [stp:stp + 32, :], None,
                                            OP.mult)
                                    else:
                                        v.scalar_tensor_tensor(
                                            c2b[:, h2, 0:256]
                                            [stp:stp + 32, :],
                                            ps[0 + h2][stp:stp + 32, 0:256],
                                            nspk[:, 2 * sc + gi2, h:h + 1]
                                            [stp:stp + 32, :],
                                            vtms[sl][:, gi2, :]
                                            [stp:stp + 32,
                                             256 * h:256 * h + 256],
                                            OP.mult, OP.add)
                                v.drain()
                                last = v.tensor_copy(twS[:, 0:1],
                                                     twS[:, 0:1])
                                last.then_inc(VV, 1)
                                cnt("VV", 1)
                                v.wait_ge(MM, smm(sc, pair, i, 4))
                                if sc > 0 and (pair, i) == (0, 0):
                                    pass
                                for h2 in range(2):
                                    v.tensor_scalar(
                                        ybuf[:, gi2, h2, 0:W]
                                        [stp:stp + 32, :],
                                        ps[0 + h2][stp:stp + 32, 0:W],
                                        spk[:, 2 * sc + gi2,
                                            2 * pair + h2:
                                            2 * pair + h2 + 1]
                                        [stp:stp + 32, :], None, OP.mult)
                                v.drain()
                                last = v.tensor_copy(twS[:, 0:1],
                                                     twS[:, 0:1])
                                last.then_inc(VV, 1)
                                cnt("VV", 1)
                                if mode == "p2":
                                    v.wait_ge(MM, smm(sc, pair, i, 6 + 1))
                                    if sc >= 2 and (pair, i) == (0, 0):
                                        v.wait_ge(OBS2[sl], ob0[sl] +
                                                  32 * (sc // 2))
                                    for h2 in range(2):
                                        h = 2 * pair + h2
                                        v.tensor_scalar(
                                            obufs[sl][:, gi2, :]
                                            [stp:stp + 32,
                                             256 * h:256 * h + 256],
                                            ps[6][stp:stp + 32,
                                                  256 * h2:256 * h2 + 256],
                                            ppk[:, 2 * sc + gi2, h:h + 1]
                                            [stp:stp + 32, :], None,
                                            OP.mult)
                                    v.drain()
                                    last = v.tensor_copy(twS[:, 0:1],
                                                         twS[:, 0:1])
                                    last.then_inc(VV, 1)
                                    cnt("VV", 1)
                        v.wait_ge(MM, smm(sc, 1, 7, 5))
                        for h2 in range(2):
                            for mc in range(2):
                                last = v.tensor_tensor(
                                    Xs[:, 4 + 2 * h2 + mc, 0:W],
                                    ps[2 + 2 * h2 + mc][:, 0:W],
                                    Xs[:, 4 + 2 * h2 + mc, 0:W], OP.add)
                                last.then_inc(VV, 1)
                                cnt("VV", 1)
                        v.drain()
                        for pair in range(2):
                            for h2 in range(2):
                                h = 2 * pair + h2
                                for mc in range(2):
                                    v.tensor_copy(Xs16[:, 2 * h + mc, 0:W],
                                                  Xs[:, 2 * h + mc, 0:W])
                        v.drain()
                        assert cnt("VV") - vv0 - VSC * sc <= VSC, (
                            mode, sc, cnt("VV") - vv0 - VSC * sc, VSC)
                        while cnt("VV") - vv0 - VSC * sc < VSC:
                            v.drain()
                            last = v.tensor_copy(twS[:, 0:1], twS[:, 0:1])
                            last.then_inc(VV, 1)
                            cnt("VV", 1)
                    return ("VV", cnt("VV"))

            if DELTA_LIM >= 1:
                delta_pass("p1")
            S.barrier()
            if "delta" in dbg:
                @S.section("scalar")
                def _(s):
                    for nm, t_ in [("tbt", tbt.rearrange("p g f -> p (g f)")),
                                   ("xs", Xs.rearrange("p s f -> p (s f)")),
                                   ("spk", spk), ("ppk2", ppk),
                                   ("bspk", bspk), ("beta2", beta_tm)]:
                        last = s.dma_start(out=dbg_aps[nm], in_=t_[:, :])
                        last.then_inc(STO, 16)
                        cnt("STO", 16)
                    return ("STO", cnt("STO"))
                S.barrier()

            if DELTA_LIM >= 2:
                mmx0 = cnt("MM")
                vvx0 = cnt("VV")

                @S.section("tensor")
                def _(t):
                    last = None
                    n = 0
                    for h in range(4):
                        for kc in range(2):
                            for mc2 in range(2):
                                if n > 0:
                                    t.wait_ge(VV, vvx0 + n)
                                last = t.transpose(
                                    ps[6][:, 0:128],
                                    Xs[:, 2 * h + kc,
                                       256 + 128 * mc2:384 + 128 * mc2],
                                    idf[:])
                                last.then_inc(MM, 1)
                                cnt("MM", 1)
                                n += 1
                    return ("MM", cnt("MM"))

                @S.section("vector")
                def _(v):
                    last = None
                    n = 0
                    for h in range(4):
                        for kc in range(2):
                            for mc2 in range(2):
                                v.wait_ge(MM, mmx0 + n + 1)
                                last = v.tensor_copy(
                                    sgm[:, 2 * h + kc, :]
                                    [:, 128 * mc2:128 * mc2 + 128],
                                    ps[6][:, 0:128])
                                last.then_inc(VV, 1)
                                cnt("VV", 1)
                                n += 1
                    return ("VV", cnt("VV"))

                @S.section("sync")
                def _(sy):
                    last = None
                    ccv = cc_in.rearrange("(s p f) -> s p f", p=128, f=512)
                    for h in range(4):
                        for kc in range(2):
                            sy.wait_ge(VV, vvx0 + 2 * (2 * h + kc) + 2)
                            last = sy.dma_start(
                                out=ccv[2 * h + kc][:, 0:256],
                                in_=Xs[:, 2 * h + kc, 0:256])
                            last.then_inc(STO, 16)
                            cnt("STO", 16)
                            last = sy.dma_start(
                                out=ccv[2 * h + kc][:, 256:512],
                                in_=sgm[:, 2 * h + kc, :])
                            last.then_inc(STO, 16)
                            cnt("STO", 16)
                    sy.wait_ge(STO, cnt("STO"))
                    return None

                S.barrier()

                @S.section("gpsimd")
                def _(g):
                    if SKIP_CC:
                        n_el = H * 2 * 128 * CCW
                        last = None
                        for rr in range(NC):
                            last = g.dma_start(
                                out=cc_out[n_el * rr:n_el * (rr + 1)],
                                in_=cc_in[:])
                        last.then_inc(CCs, 16)
                        cnt("CCs", 15)
                    else:
                        last = g.collective_compute(
                            "AllGather", OP.bypass,
                            replica_groups=[list(range(NC))],
                            ins=[cc_in.opt()],
                            outs=[cc_out.opt()])
                        last.then_inc(CCs, 1)
                    cnt("CCs", 1)
                    g.wait_ge(CCs, cnt("CCs"))
                    last = g.memset(Xs[:, :, :], 0.0)
                    last.then_inc(GG, 1)
                    cnt("GG", 1)
                    return ("GG", cnt("GG"))

                S.barrier()

                mmc0 = cnt("MM")
                vvc0 = cnt("VV")
                ldc0 = cnt("LD")
                ptb = kfb.bitcast(F32).rearrange("p (s f) -> p s f", s=8)[:, :, 0:256] \
                    if False else None
                ptbv = pool[:, 9728:11776].rearrange("p (s f) -> p s f", s=8)
                sgmv = pool[:, 11776:13824].rearrange("p (s f) -> p s f", s=8)
                clbv = pool[:, 13824:15872].rearrange("p (s f) -> p s f", s=8)

                @S.section("sync")
                def _(sy):
                    last = None
                    cco = cc_out.rearrange("(l s p f) -> l s p f", s=8,
                                           p=128, f=512)
                    for l in range(7):
                        if l >= 1:
                            sy.wait_ge(VV, vvc0 + 16 * l)
                        for sl in range(8):
                            last = sy.dma_start(
                                out=clbv[:, sl, :], in_=cco[l][sl][:, 0:256])
                            last.then_inc(LD, 16)
                            cnt("LD", 16)
                            last = sy.dma_start(
                                out=ptbv[:, sl, :], in_=cco[l][sl][:, 256:512])
                            last.then_inc(LD, 16)
                            cnt("LD", 16)
                    return ("LD", cnt("LD"))

                @S.section("tensor")
                def _(t):
                    last = None
                    for l in range(1, 7):
                        t.wait_ge(LD, ldc0 + 256 * (l + 1))
                        t.wait_ge(VV, vvc0 + 16 * l)
                        for wave in range(2):
                            if wave == 1:
                                t.wait_ge(VV, vvc0 + 16 * l + 4)
                            for sl4 in range(4):
                                sl = 4 * wave + sl4
                                h, mc = sl // 2, sl % 2
                                pp = ps[2 + sl4]
                                first = True
                                for kc in range(2):
                                    last = t.matmul(
                                        pp[:, 0:256],
                                        ptbv[:, 2 * h + kc,
                                             128 * mc:128 * mc + 128],
                                        sgmv[:, 2 * h + kc, 0:256],
                                        start=first, stop=(kc == 1))
                                    first = False
                                last.then_inc(MM, 1)
                                cnt("MM", 1)
                    return ("MM", cnt("MM"))

                @S.section("vector")
                def _(v):
                    last = None
                    for l in range(7):
                        if l == 0:
                            v.wait_ge(LD, ldc0 + 256)
                            v.wait_ge(GG, cnt("GG"))
                            for sl in range(8):
                                v.tensor_copy(sgmv[:, sl, :], clbv[:, sl, :])
                            v.drain()
                            for sl in range(8):
                                v.drain()
                                last = v.tensor_copy(twS[:, 0:1], twS[:, 0:1])
                                last.then_inc(VV, 1)
                                cnt("VV", 1)
                            for sl in range(8):
                                last = v.scalar_tensor_tensor(
                                    Xs[:, sl, 0:256], sgmv[:, sl, :],
                                    ohb[:, 1:2], Xs[:, sl, 0:256],
                                    OP.mult, OP.add)
                                last.then_inc(VV, 1)
                                cnt("VV", 1)
                            v.drain()
                        else:
                            for wave in range(2):
                                for sl4 in range(4):
                                    sl = 4 * wave + sl4
                                    v.wait_ge(MM, mmc0 + 8 * (l - 1) +
                                              (sl | 1) + 1)
                                    pp = ps[2 + sl4]
                                    v.drain()
                                    last = v.tensor_tensor(
                                        twS[:, 0:256], pp[:, 0:256],
                                        clbv[:, sl, :], OP.add)
                                    v.drain()
                                    last = v.tensor_copy(sgmv[:, sl, :],
                                                         twS[:, 0:256])
                                    last.then_inc(VV, 1)
                                    cnt("VV", 1)
                            v.drain()
                            for sl in range(8):
                                last = v.scalar_tensor_tensor(
                                    Xs[:, sl, 0:256], sgmv[:, sl, :],
                                    ohb[:, l + 1:l + 2], Xs[:, sl, 0:256],
                                    OP.mult, OP.add)
                                last.then_inc(VV, 1)
                                cnt("VV", 1)
                            v.drain()
                    for sl in range(8):
                        v.tensor_copy(Xs16[:, sl, 0:256], Xs[:, sl, 0:256])
                    v.drain()
                    last = v.tensor_copy(twS[:, 0:1], twS[:, 0:1])
                    last.then_inc(VV, 1)
                    cnt("VV", 1)
                    return ("VV", cnt("VV"))

                S.barrier()

            if DELTA_LIM >= 3:
                delta_pass("p2")
                S.barrier()
                if "delta" in dbg:
                    @S.section("gpsimd")
                    def _(g):
                        last = g.dma_start(out=dbg_aps["delta"], in_=dl_tm_h[:, :])
                        last.then_inc(LG, 16)
                        cnt("LG", 16)
                        return ("LG", cnt("LG"))
                    S.barrier()
        if "C" in phases:
            ginb = pool[:, 0:12288].bitcast(BF16).rearrange(
                "p (s c f) -> p s c f", s=2, c=24)
            wfgs = pool[:, 12288:12800].bitcast(BF16).rearrange(
                "p (s f) -> p s f", s=2)
            h1b = pool[:, 13312:15360].bitcast(BF16).rearrange(
                "p (c f) -> p c f", c=8)
            opw = pool2[:, 0:4096].bitcast(BF16).rearrange(
                "p (c f) -> p c f", c=8)
            lg_all = pool[:, 18432:18496].rearrange(
                "p (t f) -> p t f", t=16)
            wg_all = pool[:, 18496:18560].rearrange(
                "p (t f) -> p t f", t=16)
            wgm = pool[:, 18560:18656].rearrange("p (t f) -> p t f", t=16)
            ltb = pool[:, 18656:18657]
            invt = pool[:, 18657:18658]
            fgb2b = pool[:, 18658:18662]
            twS2 = pool[:, 18662:18663]
            GW = [S.sem("GWa"), S.sem("GWb")]
            GI = [S.sem("GIa"), S.sem("GIb")]
            GJ = [S.sem("GJa"), S.sem("GJb")]

            c1m0 = cnt("MM")
            c1v0 = cnt("VV")
            c1a0 = cnt("AA")
            gw0 = [cnt("GWa"), cnt("GWb")]
            gi0 = [cnt("GIa"), cnt("GIb")]
            gj0 = [cnt("GJa"), cnt("GJb")]

            def FM(tg, half, fc):
                return c1m0 + 68 * tg + 32 * half + fc + 1

            def LM(tg, ti4):
                return c1m0 + 68 * tg + 64 + ti4 + 1

            @S.section("gpsimd")
            def _(g):
                g.memset(twS2[:, :], 0.0).then_inc(GG, 1)
                cnt("GG", 1)
                lv = ltemp[:, :]
                g.dma_start(out=ltb[:, :],
                            in_=bass.AP(lv.tensor, lv.offset,
                                        [[0, 128], [1, 1]])).then_inc(LG, 16)
                cnt("LG", 16)
                fv = fgb2[:, :]
                g.dma_start(out=fgb2b[:, :],
                            in_=bass.AP(fv.tensor, fv.offset,
                                        [[0, 128], [1, 4]])).then_inc(LG, 16)
                cnt("LG", 16)
                for fc in range(8):
                    last = g.dma_start(out=opw[:, fc, :],
                                       in_=opT[128 * fc:128 * fc + 128, :])
                    last.then_inc(LG, 16)
                    cnt("LG", 16)
                for tg in range(4):
                    sl = tg % 2
                    if tg >= 2:
                        g.wait_ge(MM, FM(tg - 2, 1, 31))
                    t0 = 512 * tg
                    last = g.dma_start(
                        out=ginb[:, sl, 0:8, :],
                        in_=sh_h.rearrange("(c p) t -> p c t", p=128)
                        [:, :, t0:t0 + 512])
                    last.then_inc(GJ[sl], 16)
                    cnt("GJ" + "ab"[sl], 16)
                    last = g.dma_start(
                        out=ginb[:, sl, 8:16, :],
                        in_=lg_h.rearrange("(c p) t -> p c t", p=128)
                        [:, :, t0:t0 + 512])
                    last.then_inc(GJ[sl], 16)
                    cnt("GJ" + "ab"[sl], 16)
                g.wait_ge(LG, cnt("LG"))
                g.wait_ge(GJ[0], cnt("GJa"))
                return ("GJb", cnt("GJb"))

            @S.section("sync")
            def _(sy):
                for tg in range(4):
                    sl = tg % 2
                    if tg >= 2:
                        sy.wait_ge(MM, FM(tg - 2, 1, 31))
                    t0 = 512 * tg
                    for j in range(4):
                        for fc8 in range(8):
                            last = sy.dma_start(
                                out=ginb[:, sl, 16 + fc8,
                                         128 * j:128 * j + 128],
                                in_=dl_tm_h[t0 + 128 * j:t0 + 128 * j + 128,
                                            128 * fc8:128 * fc8 + 128],
                                transpose=True)
                            last.then_inc(GI[sl], 16)
                            cnt("GI" + "ab"[sl], 16)
                    for r in range(64):
                        w = 64 * tg + r
                        wsl = w % 2
                        half, fc = divmod(r, 32)
                        if w >= 2:
                            tg2, r2 = divmod(w - 2, 64)
                            sy.wait_ge(MM, c1m0 + 68 * tg2 + r2 + 1)
                        last = sy.dma_start(
                            out=wfgs[:, wsl, :],
                            in_=fgx_h[128 * fc:128 * fc + 128,
                                      512 * half:512 * half + 512])
                        last.then_inc(GW[wsl], 16)
                        cnt("GW" + "ab"[wsl], 16)
                sy.wait_ge(GW[0], cnt("GWa"))
                return ("GWb", cnt("GWb"))

            @S.section("tensor")
            def _(t):
                last = None
                for tg in range(4):
                    sl = tg % 2
                    t.wait_ge(GI[sl], gi0[sl] + 512 * (tg // 2 + 1))
                    t.wait_ge(GJ[sl], gj0[sl] + 32 * (tg // 2 + 1))
                    for half in range(2):
                        if 2 * tg + half > 0:
                            t.wait_ge(AA, c1a0 + 4 * (2 * tg + half))
                        for fc in range(32):
                            w = 64 * tg + 32 * half + fc
                            t.wait_ge(GW[w % 2], gw0[w % 2] +
                                      16 * (w // 2 + 1))
                            if fc < 8:
                                rhs = xn[:, fc,
                                         128 + 512 * tg:128 + 512 * tg + 512]
                            else:
                                rhs = ginb[:, sl, fc - 8, :]
                            wv = wfgs[:, w % 2, :].rearrange(
                                "p (g f) -> p g f", g=4)
                            for ghb in range(4):
                                last = t.matmul(ps[ghb][:, 0:512],
                                                wv[:, ghb, :], rhs,
                                                start=(fc == 0),
                                                stop=(fc == 31))
                            last.then_inc(MM, 1)
                            cnt("MM", 1)
                    t.wait_ge(AA, c1a0 + 8 * tg + 8)
                    for ti4 in range(4):
                        glx = 4 * tg + ti4
                        if glx >= 2:
                            t.wait_ge(VV, c1v0 + glx - 1)
                        pp = ps[4 + ti4 % 2]
                        for ghc in range(8):
                            last = t.matmul(
                                pp[:, 0:4],
                                h1b[:, ghc, 128 * ti4:128 * ti4 + 128],
                                fgw2_s[:, ghc, :],
                                start=(ghc == 0), stop=(ghc == 7))
                        last.then_inc(MM, 1)
                        cnt("MM", 1)
                return ("MM", cnt("MM"))

            @S.section("scalar")
            def _(s):
                last = None
                for tg in range(4):
                    for half in range(2):
                        if tg > 0 and half == 0:
                            s.wait_ge(MM, LM(tg - 1, 3))
                        s.wait_ge(MM, FM(tg, half, 31))
                        for ghb in range(4):
                            c = 4 * half + ghb
                            last = s.activation(h1b[:, c, :],
                                                ps[ghb][:, 0:512], AF.Gelu,
                                                bias=fgb1_s[:, c:c + 1])
                            last.then_inc(AA, 1)
                            cnt("AA", 1)
                return ("AA", cnt("AA"))

            @S.section("vector")
            def _(v):
                last = None
                for tg in range(4):
                    for ti4 in range(4):
                        v.wait_ge(MM, LM(tg, ti4))
                        last = v.tensor_copy(lg_all[:, 4 * tg + ti4, :],
                                             ps[4 + ti4 % 2][:, 0:4])
                        last.then_inc(VV, 1)
                        cnt("VV", 1)
                return ("VV", cnt("VV"))

            S.barrier()

            wa0 = cnt("AA")
            wv0 = cnt("VV")

            @S.section("scalar")
            def _(s):
                last = s.activation(invt[:, :], ltb[:, :], AF.Exp)
                last.then_inc(AA, 1)
                cnt("AA", 1)
                s.wait_ge(VV, wv0 + 1)
                s.drain()
                last = s.activation(invt[:, :], invt[:, :], AF.Ln)
                last.then_inc(AA, 1)
                cnt("AA", 1)
                s.wait_ge(VV, wv0 + 2)
                s.drain()
                last = s.activation(lg_all[:, :, :], lg_all[:, :, :], AF.Exp)
                last.then_inc(AA, 1)
                cnt("AA", 1)
                return ("AA", cnt("AA"))

            @S.section("vector")
            def _(v):
                v.wait_ge(AA, wa0 + 1)
                last = v.tensor_scalar(invt[:, :], invt[:, :], 1.0, None,
                                       OP.add)
                last.then_inc(VV, 1)
                cnt("VV", 1)
                v.wait_ge(AA, wa0 + 2)
                v.drain()
                v.tensor_scalar(invt[:, :], invt[:, :], 1e-4, None, OP.add)
                v.drain()
                v.reciprocal(invt[:, :], invt[:, :])
                v.drain()
                a0, b0 = bass.broadcast_tensor_aps(
                    lg_all[:, :, :],
                    fgb2b[:, :].rearrange("p (o f) -> p o f", o=1))
                v.tensor_tensor(lg_all[:, :, :], a0, b0, OP.add)
                v.drain()
                v.tensor_scalar(lg_all[:, :, :], lg_all[:, :, :],
                                invt[:, 0:1], None, OP.mult)
                v.drain()
                v.tensor_reduce(wgm[:, :, 0], lg_all[:, :, :], AX.X, OP.max)
                v.drain()
                a0, b0 = bass.broadcast_tensor_aps(lg_all[:, :, :],
                                                   wgm[:, :, 0:1])
                last = v.tensor_tensor(lg_all[:, :, :], a0, b0, OP.subtract)
                last.then_inc(VV, 1)
                cnt("VV", 1)
                v.wait_ge(AA, wa0 + 3)
                v.tensor_reduce(wgm[:, :, 1], lg_all[:, :, :], AX.X, OP.add)
                v.drain()
                v.reciprocal(wgm[:, :, 2], wgm[:, :, 1:2])
                v.drain()
                a0, b0 = bass.broadcast_tensor_aps(lg_all[:, :, :],
                                                   wgm[:, :, 2:3])
                v.tensor_tensor(wg_all[:, :, :], a0, b0, OP.mult)
                v.drain()
                v.tensor_scalar(wgm[:, :, 3:4], wg_all[:, :, 3:4], 0.05,
                                None, OP.mult)
                v.drain()
                a0, b0 = bass.broadcast_tensor_aps(wg_all[:, :, 0:2],
                                                   wgm[:, :, 3:4])
                v.tensor_tensor(wg_all[:, :, 0:2], a0, b0, OP.max)
                v.drain()
                v.tensor_reduce(wgm[:, :, 4], wg_all[:, :, :], AX.X, OP.add)
                v.drain()
                v.reciprocal(wgm[:, :, 5], wgm[:, :, 4:5])
                v.drain()
                a0, b0 = bass.broadcast_tensor_aps(wg_all[:, :, :],
                                                   wgm[:, :, 5:6])
                v.tensor_tensor(wg_all[:, :, :], a0, b0, OP.mult)
                v.drain()
                for fc in range(8):
                    v.tensor_scalar(opw[:, fc, :], opw[:, fc, :],
                                    onw_s[:, fc % 2:fc % 2 + 1], None,
                                    OP.mult)
                v.drain()
                last = v.tensor_copy(twS2[:, 0:1], twS2[:, 0:1])
                last.then_inc(VV, 1)
                cnt("VV", 1)
                return ("VV", cnt("VV"))

            S.barrier()
            if "cgate" in dbg:
                @S.section("scalar")
                def _(s):
                    last = s.dma_start(out=dbg_aps["cgate"],
                                       in_=wg_all[:, :, :])
                    last.then_inc(STO, 16)
                    cnt("STO", 16)
                    return ("STO", cnt("STO"))
                S.barrier()

            btm = pool[:, 0:4096].bitcast(BF16).rearrange(
                "p (s b f) -> p s b f", s=2, b=4)
            ofm = pool[:, 4096:6144].bitcast(BF16).rearrange(
                "p (c f) -> p c f", c=8)
            obc = pool[:, 6144:7168]
            obn = pool[:, 7168:7680].bitcast(BF16)
            sqb = pool[:, 7680:8704]
            rsm = pool[:, 18664:18672]
            BT = [S.sem("BTa"), S.sem("BTb")]
            bt0 = [cnt("BTa"), cnt("BTb")]
            c2m0 = cnt("MM")
            c2v0 = cnt("VV")
            c2a0 = cnt("AA")
            CM = {}
            CV = {}
            CA = {}
            _m = _v = _a = 0
            for ti in range(16):
                _v += 1
                CV[("cmb", ti)] = _v
                _a += 1
                CA[("rst", ti)] = _a
                _v += 1
                CV[("obn", ti)] = _v
                for fc in range(8):
                    _m += 1
                    CM[("tr", ti, fc)] = _m
                    _v += 1
                    CV[("evc", ti, fc)] = _v
                if ti % 4 == 3:
                    tg = ti // 4
                    for db in range(8):
                        _m += 1
                        CM[("op", tg, db)] = _m
                    for db in range(8):
                        _v += 1
                        CV[("acc", tg, db)] = _v

            @S.section("sync")
            def _(sy):
                for ti in range(16):
                    sl = ti % 2
                    if ti >= 2:
                        sy.wait_ge(VV, c2v0 + CV[("cmb", ti - 2)])
                    t0 = 128 * ti
                    last = sy.dma_start(out=btm[:, sl, 0, :],
                                        in_=sh_h[:, t0:t0 + 128],
                                        transpose=True)
                    last.then_inc(BT[sl], 16)
                    cnt("BT" + "ab"[sl], 16)
                    last = sy.dma_start(out=btm[:, sl, 1, :],
                                        in_=lg_h[:, t0:t0 + 128],
                                        transpose=True)
                    last.then_inc(BT[sl], 16)
                    cnt("BT" + "ab"[sl], 16)
                    last = sy.dma_start(out=btm[:, sl, 2, :],
                                        in_=dl_tm_h[t0:t0 + 128, :])
                    last.then_inc(BT[sl], 16)
                    cnt("BT" + "ab"[sl], 16)
                    last = sy.dma_start(out=btm[:, sl, 3, :],
                                        in_=vf_h[:, t0 + 32:t0 + 160],
                                        transpose=True)
                    last.then_inc(BT[sl], 16)
                    cnt("BT" + "ab"[sl], 16)
                sy.wait_ge(BT[0], cnt("BTa"))
                return ("BTb", cnt("BTb"))

            @S.section("vector")
            def _(v):
                last = None
                for ti in range(16):
                    sl = ti % 2
                    v.wait_ge(BT[sl], bt0[sl] + 64 * (ti // 2 + 1))
                    v.tensor_scalar(obc[:, :], btm[:, sl, 0, :],
                                    wg_all[:, ti, 0:1], None, OP.mult)
                    for b in range(1, 4):
                        v.drain()
                        v.scalar_tensor_tensor(obc[:, :], btm[:, sl, b, :],
                                               wg_all[:, ti, b:b + 1],
                                               obc[:, :], OP.mult, OP.add)
                    v.drain()
                    v.tensor_tensor(sqb[:, :], obc[:, :], obc[:, :], OP.mult)
                    v.drain()
                    v.tensor_reduce(rsm[:, 0:4],
                                    sqb.rearrange("p (h f) -> p h f", h=4),
                                    AX.X, OP.add)
                    v.drain()
                    last = v.tensor_scalar(rsm[:, 4:8], rsm[:, 0:4],
                                           1.0 / 256, 1e-5, OP.mult,
                                           op1=OP.add)
                    last.then_inc(VV, 1)
                    cnt("VV", 1)
                    v.wait_ge(AA, c2a0 + CA[("rst", ti)])
                    if ti >= 1:
                        v.wait_ge(MM, c2m0 + CM[("tr", ti - 1, 7)])
                    for h in range(4):
                        v.tensor_scalar(obn[:, 256 * h:256 * h + 256],
                                        obc[:, 256 * h:256 * h + 256],
                                        rsm[:, 4 + h:5 + h], None, OP.mult)
                    v.drain()
                    last = v.tensor_copy(twS2[:, 0:1], twS2[:, 0:1])
                    last.then_inc(VV, 1)
                    cnt("VV", 1)
                    for fc in range(8):
                        v.wait_ge(MM, c2m0 + CM[("tr", ti, fc)])
                        if ti % 4 == 0 and ti >= 4 and fc == 0:
                            v.wait_ge(MM, c2m0 + CM[("op", ti // 4 - 1, 7)])
                        last = v.tensor_copy(
                            ofm[:, fc, 128 * (ti % 4):128 * (ti % 4) + 128],
                            pt16[0][:, 0:128])
                        last.then_inc(VV, 1)
                        cnt("VV", 1)
                    if ti % 4 == 3:
                        tg = ti // 4
                        for db in range(8):
                            v.wait_ge(MM, c2m0 + CM[("op", tg, db)])
                            v.drain()
                            last = v.tensor_tensor(
                                acc[:, db, 512 * tg:512 * tg + 512],
                                ps[2 + db % 2][:, 0:512],
                                acc[:, db, 512 * tg:512 * tg + 512], OP.add)
                            last.then_inc(VV, 1)
                            cnt("VV", 1)
                return ("VV", cnt("VV"))

            @S.section("scalar")
            def _(s):
                last = None
                for ti in range(16):
                    s.wait_ge(VV, c2v0 + CV[("cmb", ti)])
                    s.drain()
                    s.activation(rsm[:, 4:8], rsm[:, 4:8], AF.Ln)
                    s.drain()
                    last = s.activation(rsm[:, 4:8], rsm[:, 4:8], AF.Exp,
                                        scale=-0.5)
                    last.then_inc(AA, 1)
                    cnt("AA", 1)
                return ("AA", cnt("AA"))

            @S.section("tensor")
            def _(t):
                last = None
                for ti in range(16):
                    t.wait_ge(VV, c2v0 + CV[("obn", ti)])
                    for fc in range(8):
                        if (ti, fc) != (0, 0):
                            pfc = fc - 1 if fc > 0 else 7
                            pti = ti if fc > 0 else ti - 1
                            t.wait_ge(VV, c2v0 + CV[("evc", pti, pfc)])
                        last = t.transpose(pt16[0][:, 0:128],
                                           obn[:, 128 * fc:128 * fc + 128],
                                           id16[:])
                        last.then_inc(MM, 1)
                        cnt("MM", 1)
                    if ti % 4 == 3:
                        tg = ti // 4
                        t.wait_ge(VV, c2v0 + CV[("evc", ti, 7)])
                        for db in range(8):
                            if tg >= 1:
                                t.wait_ge(VV, c2v0 +
                                          CV[("acc", tg - 1,
                                              db - 2 if db >= 2 else
                                              6 + db)])
                            elif db >= 2:
                                t.wait_ge(VV, c2v0 + CV[("acc", 0, db - 2)]
                                          if False else c2v0)
                            if db >= 2:
                                t.wait_ge(VV, c2v0 + CV[("acc", tg, db - 2)])
                            for fc in range(8):
                                last = t.matmul(
                                    ps[2 + db % 2][:, 0:512],
                                    opw[:, fc, 128 * db:128 * db + 128],
                                    ofm[:, fc, :],
                                    start=(fc == 0), stop=(fc == 7))
                            last.then_inc(MM, 1)
                            cnt("MM", 1)
                return ("MM", cnt("MM"))

            S.barrier()

        if "B" in phases and "A" in phases:
            emit_B2()

        if "D" in phases:
            xrb = pool[:, 0:2048].rearrange("p (b t) -> p b t", b=2)
            osb = pool[:, 2048:6144].rearrange("p (b t) -> p b t", b=2)
            vvd0 = cnt("VV")
            mmd0 = cnt("MM")
            xsd0 = [cnt("XSa"), cnt("XSb")]
            osd0 = [cnt("OSa"), cnt("OSb")]

            @S.section("sync")
            def _(sy):
                last = None
                for i in range(NTOWN):
                    b = i % 2
                    if i >= 2:
                        sy.wait_ge(VV, vvd0 + 8 * (i - 1))
                    last = sy.dma_start(out=xrb[:, b, :],
                                        in_=x_ext[ts(i + 1), :])
                    last.then_inc(XS[b], 16)
                    cnt("XS" + "ab"[b], 16)
                return ("XS" + "ab"[(NTOWN - 1) % 2], cnt("XS" + "ab"[(NTOWN - 1) % 2]))

            @S.section("tensor")
            def _(t):
                last = None
                for i in range(NTOWN):
                    for c in range(8):
                        if 8 * i + c >= 2:
                            t.wait_ge(VV, vvd0 + 8 * i + c - 2 + 1)
                        last = t.transpose(ps[c % 2][:, 0:128], acc[:, c, ts(i)],
                                           idf[:])
                        last.then_inc(MM, 1)
                cnt("MM", 8 * NTOWN)
                return ("MM", cnt("MM"))

            @S.section("vector")
            def _(v):
                last = None
                for i in range(NTOWN):
                    b = i % 2
                    v.wait_ge(XS[b], xsd0[b] + 16 * (i // 2 + 1))
                    if i >= 2:
                        v.wait_ge(OS[b], osd0[b] + 16 * (i // 2))
                    for c in range(8):
                        v.wait_ge(MM, mmd0 + 8 * i + c + 1)
                        last = v.tensor_tensor(osb[:, b, ts(c)],
                                               ps[c % 2][:, 0:128],
                                               xrb[:, b, ts(c)], OP.add)
                        last.then_inc(VV, 1)
                cnt("VV", 8 * NTOWN)
                return ("VV", cnt("VV"))

            @S.section("scalar")
            def _(s):
                last = None
                for i in range(NTOWN):
                    b = i % 2
                    s.wait_ge(VV, vvd0 + 8 * (i + 1))
                    last = s.dma_start(out=out[ts(i), :], in_=osb[:, b, 0:1024])
                    last.then_inc(OS[b], 16)
                    cnt("OS" + "ab"[b], 16)
                s.wait_ge(OS[0], cnt("OSa"))
                s.wait_ge(OS[1], cnt("OSb"))
                return None

        S.barrier()
    return nc, dram, dbg_aps



def _diag_expand(w, taps):
    out = np.zeros((128, 8, taps, 128), np.float32)
    for c in range(8):
        for t in range(taps):
            np.fill_diagonal(out[:, c, t, :], w[c * 128:(c + 1) * 128, t])
    return out


def prep_inputs(inputs, core):
    g = lambda k: np.asarray(inputs[k])
    x = g("L0_attn_hidden_states")[0]
    lo = core * TC
    x_ext = np.zeros((TEXT, D), np.float32)
    src_lo = lo - HALO
    pad = max(0, -src_lo)
    x_ext[pad:] = x[src_lo + pad: lo + TC]
    m = {"x_ext": np.ascontiguousarray(x_ext)}

    tr = lambda a: np.ascontiguousarray(np.asarray(a).T.astype(np.float32))
    m["wqT"] = tr(g("L0_attn_q_proj_w"))
    m["wkT"] = tr(g("L0_attn_k_proj_w"))
    m["wvT"] = tr(g("L0_attn_v_proj_w"))
    m["wbT"] = tr(g("L0_attn_b_proj_w"))
    m["qcd"] = _diag_expand(g("L0_attn_q_conv_w"), 4)
    m["kcd"] = _diag_expand(g("L0_attn_k_conv_w"), 4)
    m["vcd"] = _diag_expand(g("L0_attn_v_conv_w"), 4)
    m["fsd"] = _diag_expand(g("L0_attn_fir_short_w").reshape(D, 3), 3)
    m["fld"] = _diag_expand(g("L0_attn_fir_long_w").reshape(D, 31), 31)
    m["fgw1T"] = tr(g("L0_attn_fg_w1"))
    m["fgb1"] = np.ascontiguousarray(
        g("L0_attn_fg_b1").reshape(8, 128).T.astype(np.float32))
    m["fgw2T"] = tr(g("L0_attn_fg_w2"))
    m["fgb2"] = g("L0_attn_fg_b2").reshape(1, 4).astype(np.float32)
    m["ltemp"] = g("L0_attn_logit_temp").reshape(1, 1).astype(np.float32)
    m["onw"] = np.ascontiguousarray(
        g("L0_attn_o_norm_w").reshape(2, 128).T.astype(np.float32))
    m["opT"] = tr(g("L0_attn_o_proj_w"))
    m["gwT"] = tr(g("L0_ffn_gate_w"))
    m["gb"] = g("L0_ffn_gate_b").reshape(1, E).astype(np.float32)
    m["w1T"] = np.ascontiguousarray(
        np.asarray(g("L0_ffn_w1")).transpose(0, 2, 1).astype(np.float32))
    m["w3T"] = np.ascontiguousarray(
        np.asarray(g("L0_ffn_w3")).transpose(0, 2, 1).astype(np.float32))
    m["w2T"] = np.ascontiguousarray(
        np.asarray(g("L0_ffn_w2")).transpose(0, 2, 1).astype(np.float32))
    m["b1r"] = np.ascontiguousarray(
        g("L0_ffn_b1").reshape(E, 8, 128).transpose(2, 0, 1).astype(np.float32))
    m["b3r"] = np.ascontiguousarray(
        g("L0_ffn_b3").reshape(E, 8, 128).transpose(2, 0, 1).astype(np.float32))
    m["ws1T"] = tr(g("L0_ffn_ws1"))
    m["ws3T"] = tr(g("L0_ffn_ws3"))
    m["ws2T"] = tr(g("L0_ffn_ws2"))
    m["bs1r"] = np.ascontiguousarray(
        g("L0_ffn_bs1").reshape(16, 128).T.astype(np.float32))
    m["bs3r"] = np.ascontiguousarray(
        g("L0_ffn_bs3").reshape(16, 128).T.astype(np.float32))
    b2a = np.zeros((E + 1, D), np.float32)
    b2a[:E] = g("L0_ffn_b2")
    b2a[E] = g("L0_ffn_bs2")
    m["b2aug"] = b2a
    m["ident"] = np.eye(128, dtype=bf)
    m["identf"] = np.eye(128, dtype=np.float32)
    m["qeye"] = (np.eye(128) * 0.25).astype(np.float32)
    hs = np.zeros((128, 16, 8), np.float32)
    for c in range(8):
        hs[:, c, c // 2] = 1.0
        hs[:, 8 + c, 4 + c // 2] = 1.0
    m["hsel"] = hs
    m["trilm"] = np.tril(np.ones((32, 32), np.float32))
    tau = np.arange(32)
    tr_s = (tau[None, :] > tau[:, None]).astype(np.float32)
    tr_i = (tau[None, :] >= tau[:, None]).astype(np.float32)
    m["trius"] = np.tile(tr_s, (4, 1)).astype(np.float32)
    m["triuqk"] = np.tile(tr_i, (4, 1)).astype(np.float32)
    m["trils"] = np.tile(tr_s.T, (4, 1)).astype(np.float32)
    cmsk = np.zeros((128, 2048), np.float32)
    tri_incl = (np.arange(32)[:, None] <= np.arange(32)[None, :])
    for jg in range(2):
        gm = np.zeros((128, 256), np.float32)
        qm = np.zeros((128, 256), np.float32)
        for cs in range(4):
            j = 4 * jg + cs
            for ig in range(8):
                if j < ig:
                    gm[32 * cs:32 * cs + 32, 32 * ig:32 * ig + 32] = 1.0
                    qm[32 * cs:32 * cs + 32, 32 * ig:32 * ig + 32] = 1.0
                elif j == ig:
                    qm[32 * cs:32 * cs + 32,
                       32 * ig:32 * ig + 32] = tri_incl
        cmsk[:, 512 * jg:512 * jg + 256] = gm
        cmsk[:, 512 * jg + 256:512 * jg + 512] = gm
        cmsk[:, 1024 + 512 * jg:1024 + 512 * jg + 256] = qm
        cmsk[:, 1024 + 512 * jg + 256:1024 + 512 * jg + 512] = qm
    m["cmask"] = cmsk
    m["eye4"] = np.tile(np.eye(32, dtype=np.float32), (4, 1))
    oh = np.zeros((128, 8), np.float32)
    oh[:, core] = 1.0
    m["onehot"] = oh
    return m


_BUILD_CACHE = {}


def kernel(**inputs):
    key = ()
    if key not in _BUILD_CACHE:
        _BUILD_CACHE[key] = build_nc()
    nc, dram, dbg_aps = _BUILD_CACHE[key]
    in_maps = [prep_inputs(inputs, c) for c in range(NC)]
    res = run_bass_kernel_spmd(nc, in_maps, core_ids=list(range(NC)))
    outp = np.concatenate([res.results[c]["out"] for c in range(NC)], axis=0)
    return outp.reshape(1, T, D).astype(np.float32)
